# version 26
# baseline (speedup 1.0000x reference)
import numpy as np
import ml_dtypes
import concourse.bass as bass
import concourse.mybir as mybir
from concourse.bass_utils import run_bass_kernel_spmd
from contextlib import ExitStack

F32 = mybir.dt.float32
BF16 = mybir.dt.bfloat16
I32 = mybir.dt.int32
F8 = mybir.dt.float8e4
AF = mybir.ActivationFunctionType
OP = mybir.AluOpType
AX = mybir.AxisListType

D = 2048
H = 16
KVH = 4
DH = 128
TOWN = 1024
TKV = 2048
NT_OWN = 8
NT_KV = 16
NDC = 16
MSOFT = 12.0
LNEPS = 1e-5
RMSEPS = 1e-6
QSCALE = 0.08838834764831845
LNQS = -2.4259509524980262
MAGIC = 12582912.0
KILL = -30000.0
NCORE = 8


class S:
    def __init__(self, st, nc, name):
        self.h = st.enter_context(nc.semaphore(name))
        self.n = 0

    def inc(self, instr, k=1):
        instr.then_inc(self.h, k)
        self.n += k
        return instr


def kt_geom(kt):
    if kt <= 7:
        return 0, 512, 0, 3
    return 128 * (kt - 8), 512 - 128 * (kt - 8), kt - 8, 3


def build():
    nc = bass.Bass("TRN2", target_bir_lowering=False)
    ap = {}

    def din(name, shape, dt=F32):
        ap[name] = nc.dram_tensor(name, shape, dt, kind="ExternalInput").ap()

    din("xTt", [NT_KV, D, 128])
    din("xown", [TOWN, D])
    din("wqT", [D, D])
    din("wkT", [D, KVH * DH])
    din("wvT", [D, KVH * DH])
    din("woT", [D, D])
    din("wffn", [D, D])
    din("fbias", [1, D])
    din("costab", [TKV, 64])
    din("sintab", [TKV, 64])
    din("expbias", [1, 28])
    din("qtab", [1, 64])
    out = nc.dram_tensor("out", [TOWN, D], F32, kind="ExternalOutput").ap()
    wq8_dram = nc.dram_tensor("wq8_dram", [D, D], BF16, kind="Internal").ap()

    st = ExitStack()
    sA = ExitStack()
    sB = ExitStack()
    sC = ExitStack()
    sD = ExitStack()
    sE = ExitStack()
    with st:
        e = st.enter_context
        ident = e(nc.sbuf_tensor("ident", [128, 128], BF16))
        maskT1 = e(nc.sbuf_tensor("maskT1", [128, 512], BF16))
        mask0 = e(nc.sbuf_tensor("mask0", [128, 512], BF16))
        onescol = e(nc.sbuf_tensor("onescol", [1, 128], BF16))
        expb = e(nc.sbuf_tensor("expb", [128, 28], F32))
        qtabs = e(nc.sbuf_tensor("qtabs", [128, 64], F32))
        mgc = e(nc.sbuf_tensor("mgc", [128, 1], F32))
        mini = e(nc.sbuf_tensor("mini", [128, 64], F32))
        sqb = e(nc.sbuf_tensor("sqb", [128, D], F32))

        sems = {}
        for name in ("sd_misc sd_bias sd_tab sd_x0 sd_x1 sd_wkv sd_wq sd_wo sd_xo0 sd_xo1 sd_wf0 sd_wf1 sd_out0 sd_out1 sd_wq80 sd_wq81 sd_wqL s_pre s_mm s_ev "
                     "s_sq s_red s_act s_rope s_tr s_trev s_smm s_msk s_exp s_avk s_ystop "
                     "s_rcp s_yev s_z s_nb s_ln s_qw s_qa s_fmm s_z2 s_nb2 s_ln2").split():
            sems[name] = S(st, nc, name)
        g = lambda n: sems[n]
        blk = st.enter_context(nc.Block())

        if True:
            @blk.gpsimd
            def _(gp):
                gp.memset(ident[:, :], 0.0)
                gp.memset(maskT1[:, :], 0.0)
                gp.memset(mask0[:, :], 0.0)
                gp.drain()
                gp.memset(mask0[0:1, :], -10000.0)
                gp.memset(onescol[:, :], 1.0)
                gp.memset(mgc[:, :], 1.0)
                gp.drain()
                gp.tensor_scalar(mgc[:, :], mgc[:, :], 1e-9, -1.0, op0=OP.add, op1=OP.add)
                gp.drain()
                gp.tensor_scalar(mgc[:, :], mgc[:, :], 6.755399441055744e+24, MAGIC,
                                 op0=OP.mult, op1=OP.add)
                gp.drain()
                gp.affine_select(out=ident[:, :], in_=ident[:, :], compare_op=OP.not_equal,
                                 fill=1.0, base=0, pattern=[[-1, 128]], channel_multiplier=1)
                gp.drain()
                gp.affine_select(out=maskT1[:, 0:128], in_=maskT1[:, 0:128], compare_op=OP.is_ge,
                                 fill=-10000.0, base=0, pattern=[[1, 128]], channel_multiplier=-1)
                g("s_pre").inc(gp.drain())
                g("sd_misc").inc(gp.dma_start(out=expb[:, :],
                                              in_=ap["expbias"][0:1, :].broadcast_to((128, 28))), 16)
                g("sd_misc").inc(gp.dma_start(out=qtabs[:, :],
                                              in_=ap["qtab"][0:1, :].broadcast_to((128, 64))), 16)

        eB = sB.enter_context
        Qfm = eB(nc.sbuf_tensor("Qfm", [128, H * TOWN], BF16, side="right"))
        Kfm = eB(nc.sbuf_tensor("Kfm", [128, KVH * TKV], BF16, side="right"))
        Vaug = eB(nc.sbuf_tensor("Vaug", [128, KVH * NT_KV * 129], BF16, side="right"))

        eA = sA.enter_context
        x_tile = [eA(nc.sbuf_tensor(f"x_tile{i}", [128, NDC * 128], BF16, side="right")) for i in range(2)]
        wkT_sb = eA(nc.sbuf_tensor("wkT_sb", [128, NDC * 512], BF16, side="right"))
        wvT_sb = eA(nc.sbuf_tensor("wvT_sb", [128, NDC * 512], BF16, side="right"))
        wqT_sb = eA(nc.sbuf_tensor("wqT_sb", [128, NDC * D], BF16, side="right"))
        cos_sb = eA(nc.sbuf_tensor("cos_sb", [128, NT_KV * 64], F32, side="right"))
        sin_sb = eA(nc.sbuf_tensor("sin_sb", [128, NT_KV * 64], F32, side="right"))
        k_rope = [eA(nc.sbuf_tensor(f"k_rope{i}", [128, 512], BF16, side="right")) for i in range(2)]
        q_rope = [eA(nc.sbuf_tensor(f"q_rope{i}", [128, D], BF16, side="right")) for i in range(2)]
        rtmp1 = eA(nc.sbuf_tensor("rtmp1", [128, 1024], F32, side="right"))
        rtmp2 = eA(nc.sbuf_tensor("rtmp2", [128, 1024], F32, side="right"))

        if True:
            @blk.gpsimd
            def _(gp):
                va = Vaug[:, :].rearrange("p (gg t c) -> p (gg t) c", gg=KVH, c=129)
                gp.memset(va[:, :, 128:129], 1.0)
                g("s_pre").inc(gp.drain())
                for dc in range(NDC):
                    g("sd_wkv").inc(gp.dma_start(out=wkT_sb[:, 512 * dc:512 * (dc + 1)],
                                                 in_=ap["wkT"][128 * dc:128 * (dc + 1), :]), 16)
                    g("sd_wkv").inc(gp.dma_start(out=wvT_sb[:, 512 * dc:512 * (dc + 1)],
                                                 in_=ap["wvT"][128 * dc:128 * (dc + 1), :]), 16)
                for tt in range(NT_KV):
                    if tt >= 2:
                        gp.wait_ge(g("s_mm").h, 2 * (tt - 1))
                    dst = x_tile[tt % 2][:, :].rearrange("p (c j) -> p c j", j=128)
                    src = bass.AP(ap["xTt"].tensor, 128 * 2048 * tt, [[128, 128], [128 * 128, NDC], [1, 128]])
                    g(f"sd_x{tt % 2}").inc(gp.dma_start(out=dst, in_=src), 16)
                for dc in range(NDC):
                    g("sd_wq").inc(gp.dma_start(out=wqT_sb[:, 2048 * dc:2048 * (dc + 1)],
                                                in_=ap["wqT"][128 * dc:128 * (dc + 1), :]), 16)
                for tt in range(NT_OWN):
                    gp.wait_ge(g("s_mm").h, 32 + max(0, 4 * (tt - 1)))
                    dst = x_tile[tt % 2][:, :].rearrange("p (c j) -> p c j", j=128)
                    src = bass.AP(ap["xTt"].tensor, 128 * 2048 * (8 + tt), [[128, 128], [128 * 128, NDC], [1, 128]])
                    g(f"sd_x{tt % 2}").inc(gp.dma_start(out=dst, in_=src), 16)

            @blk.sync
            def _(sy):
                ct = cos_sb[:, :].rearrange("p (t c) -> p t c", c=64)
                stt = sin_sb[:, :].rearrange("p (t c) -> p t c", c=64)
                g("sd_tab").inc(sy.dma_start(out=ct, in_=ap["costab"].rearrange("(t p) c -> p t c", p=128)), 16)
                g("sd_tab").inc(sy.dma_start(out=stt, in_=ap["sintab"].rearrange("(t p) c -> p t c", p=128)), 16)

        def rope_ps(v, src_t, pitch, nh, tt_kv, zsl, dst):
            x1 = bass.AP(src_t, 0, [[pitch, 128], [128, nh], [2, 64]])
            x2 = bass.AP(src_t, 1, [[pitch, 128], [128, nh], [2, 64]])
            cosv = bass.AP(cos_sb, 64 * tt_kv, [[NT_KV * 64, 128], [0, nh], [1, 64]])
            sinv = bass.AP(sin_sb, 64 * tt_kv, [[NT_KV * 64, 128], [0, nh], [1, 64]])
            t1 = bass.AP(rtmp1, 0, [[1024, 128], [64, nh], [1, 64]])
            t2 = bass.AP(rtmp2, 0, [[1024, 128], [64, nh], [1, 64]])
            zb = zsl.unsqueeze(2).broadcast_to((128, nh, 64))
            o1 = bass.AP(dst, 0, [[nh * 128, 128], [128, nh], [1, 64]])
            o2 = bass.AP(dst, 64, [[nh * 128, 128], [128, nh], [1, 64]])
            v.tensor_tensor(t1, x1, cosv, op=OP.mult)
            v.tensor_tensor(t2, x2, sinv, op=OP.mult)
            v.drain()
            v.tensor_tensor(t1, t1, t2, op=OP.subtract)
            v.drain()
            v.tensor_tensor(o1, t1, zb, op=OP.mult)
            v.tensor_tensor(t2, x2, cosv, op=OP.mult)
            v.drain()
            v.tensor_tensor(t1, x1, sinv, op=OP.mult)
            v.drain()
            v.tensor_tensor(t1, t1, t2, op=OP.add)
            v.drain()
            v.tensor_tensor(o2, t1, zb, op=OP.mult)

        with ExitStack() as p1t:
            psT = [p1t.enter_context(nc.psum_tensor(f"psT{i}", [128, 128], BF16)) for i in range(2)]

            def k_transpose_group(t, j):
                t.wait_ge(g("s_rope").h, j + 1)
                for g4 in range(4):
                    idx = 4 * j + g4
                    if idx >= 2:
                        t.wait_ge(g("s_trev").h, idx - 1)
                    g("s_tr").inc(t.transpose(psT[idx % 2][:, :],
                                              k_rope[j % 2][:, 128 * g4:128 * (g4 + 1)], ident[:, :]))

            def k_trev_group(s, j):
                for g4 in range(4):
                    idx = 4 * j + g4
                    s.wait_ge(g("s_tr").h, idx + 1)
                    s.activation(Kfm[:, TKV * g4 + 128 * j:TKV * g4 + 128 * (j + 1)],
                                 psT[idx % 2][:, :], AF.Copy)
                    g("s_trev").inc(s.drain())

            with ExitStack() as p1a:
                pe = p1a.enter_context
                psK = [pe(nc.psum_tensor(f"psK{i}", [128, 512], F32)) for i in range(2)]
                psV = [pe(nc.psum_tensor(f"psV{i}", [128, 512], F32)) for i in range(2)]

                if True:
                    @blk.tensor
                    def _(t):
                        t.wait_ge(g("sd_wkv").h, 16 * 2 * NDC)
                        t.wait_ge(g("s_pre").h, 1)
                        for tt in range(NT_KV):
                            buf = tt % 2
                            if tt >= 2:
                                t.wait_ge(g("s_rope").h, tt - 1)
                                t.wait_ge(g("s_ev").h, tt - 1)
                            t.wait_ge(g(f"sd_x{tt % 2}").h, 16 * (tt // 2 + 1))
                            ins = None
                            for dc in range(NDC):
                                ins = t.matmul(psK[buf][:, :], x_tile[buf][:, 128 * dc:128 * (dc + 1)],
                                               wkT_sb[:, 512 * dc:512 * (dc + 1)],
                                               start=(dc == 0), stop=(dc == NDC - 1))
                            g("s_mm").inc(ins)
                            for dc in range(NDC):
                                ins = t.matmul(psV[buf][:, :], x_tile[buf][:, 128 * dc:128 * (dc + 1)],
                                               wvT_sb[:, 512 * dc:512 * (dc + 1)],
                                               start=(dc == 0), stop=(dc == NDC - 1))
                            g("s_mm").inc(ins)
                            if tt >= 1:
                                k_transpose_group(t, tt - 1)
                        k_transpose_group(t, NT_KV - 1)

                    @blk.scalar
                    def _(s):
                        s.wait_ge(g("sd_misc").h, 32)
                        for tt in range(NT_KV):
                            buf = tt % 2
                            s.wait_ge(g("s_mm").h, 2 * tt + 1)
                            s.activation(sqb[:, 0:512], psK[buf][:, :], AF.Square)
                            g("s_sq").inc(s.drain())
                            s.wait_ge(g("s_mm").h, 2 * tt + 2)
                            dstv = bass.AP(Vaug, 129 * tt, [[KVH * NT_KV * 129, 128], [NT_KV * 129, KVH], [1, 128]])
                            srcv = bass.AP(psV[buf], 0, [[512, 128], [128, KVH], [1, 128]])
                            s.activation(dstv, srcv, AF.Copy)
                            g("s_ev").inc(s.drain())
                            s.wait_ge(g("s_red").h, tt + 1)
                            s.activation(mini[:, 8:12], mini[:, 0:4], AF.Ln,
                                         scale=expb[:, 27:28], bias=expb[:, 25:26])
                            s.drain()
                            s.activation(mini[:, 12:16], mini[:, 8:12], AF.Exp, scale=-0.5)
                            g("s_act").inc(s.drain())
                            if tt >= 1:
                                k_trev_group(s, tt - 1)
                        k_trev_group(s, NT_KV - 1)

                    @blk.vector
                    def _(v):
                        v.wait_ge(g("sd_tab").h, 32)
                        for tt in range(NT_KV):
                            buf = tt % 2
                            v.wait_ge(g("s_sq").h, tt + 1)
                            v.tensor_reduce(mini[:, 0:4], sqb[:, 0:512].rearrange("p (h d) -> p h d", d=128),
                                            axis=AX.X, op=OP.add)
                            g("s_red").inc(v.drain())
                            v.wait_ge(g("s_act").h, tt + 1)
                            if tt >= 2:
                                v.wait_ge(g("s_tr").h, 4 * (tt - 1))
                            rope_ps(v, psK[buf], 512, 4, tt, mini[:, 12:16], k_rope[buf])
                            g("s_rope").inc(v.drain())

            bmm, bsq = g("s_mm").n, g("s_sq").n
            bred, bact, brope = g("s_red").n, g("s_act").n, g("s_rope").n
            btr, btrev = g("s_tr").n, g("s_trev").n

            with ExitStack() as p1b:
                psQ = p1b.enter_context(nc.psum_tensor("psQ", [128, 2048], F32))

                def q_transpose_group(t, j):
                    t.wait_ge(g("s_rope").h, brope + j + 1)
                    for g4 in range(16):
                        idx = 16 * j + g4
                        if idx >= 2:
                            t.wait_ge(g("s_trev").h, btrev + idx - 1)
                        g("s_tr").inc(t.transpose(psT[idx % 2][:, :],
                                                  q_rope[j % 2][:, 128 * g4:128 * (g4 + 1)], ident[:, :]))

                def q_trev_group(s, j):
                    for g4 in range(16):
                        idx = 16 * j + g4
                        s.wait_ge(g("s_tr").h, btr + idx + 1)
                        s.activation(Qfm[:, TOWN * g4 + 128 * j:TOWN * g4 + 128 * (j + 1)],
                                     psT[idx % 2][:, :], AF.Copy)
                        g("s_trev").inc(s.drain())

                if True:
                    @blk.tensor
                    def _(t):
                        t.wait_ge(g("sd_wq").h, 16 * NDC)
                        for tt in range(NT_OWN):
                            buf = tt % 2
                            if tt >= 1:
                                t.wait_ge(g("s_rope").h, brope + tt)
                            t.wait_ge(g(f"sd_x{tt % 2}").h, 16 * (8 + tt // 2 + 1))
                            for og in range(4):
                                ins = None
                                for dc in range(NDC):
                                    ins = t.matmul(psQ[:, 512 * og:512 * (og + 1)],
                                                   x_tile[buf][:, 128 * dc:128 * (dc + 1)],
                                                   wqT_sb[:, 2048 * dc + 512 * og:2048 * dc + 512 * (og + 1)],
                                                   start=(dc == 0), stop=(dc == NDC - 1))
                                g("s_mm").inc(ins)
                            if tt >= 1:
                                q_transpose_group(t, tt - 1)
                        q_transpose_group(t, NT_OWN - 1)

                    @blk.scalar
                    def _(s):
                        s.wait_ge(g("sd_misc").h, 32)
                        for tt in range(NT_OWN):
                            s.wait_ge(g("s_mm").h, bmm + 4 * tt + 4)
                            s.activation(sqb[:, :], psQ[:, :], AF.Square)
                            g("s_sq").inc(s.drain())
                            s.wait_ge(g("s_red").h, bred + tt + 1)
                            s.activation(mini[:, 16:32], mini[:, 0:16], AF.Ln, bias=expb[:, 24:25])
                            s.drain()
                            s.activation(mini[:, 48:64], mini[:, 16:32], AF.Exp, scale=-0.5)
                            g("s_act").inc(s.drain())
                            if tt >= 1:
                                q_trev_group(s, tt - 1)
                        q_trev_group(s, NT_OWN - 1)

                    @blk.vector
                    def _(v):
                        for tt in range(NT_OWN):
                            buf = tt % 2
                            v.wait_ge(g("s_sq").h, bsq + tt + 1)
                            v.tensor_reduce(mini[:, 0:16], sqb[:, :].rearrange("p (h d) -> p h d", d=128),
                                            axis=AX.X, op=OP.add)
                            g("s_red").inc(v.drain())
                            v.wait_ge(g("s_act").h, bact + tt + 1)
                            if tt >= 2:
                                v.wait_ge(g("s_tr").h, btr + 16 * (tt - 1))
                            rope_ps(v, psQ, 2048, 16, 8 + tt, mini[:, 48:64], q_rope[buf])
                            g("s_rope").inc(v.drain())
        sA.close()

        def make_quant_scratch(pe, tag, cw=512, mode="A"):
            nb = cw // 32
            sc = {"nb": nb, "cw": cw}
            sc["qq"] = pe(nc.sbuf_tensor(f"qq{tag}", [128, cw], F32))
            sc["qsq"] = pe(nc.sbuf_tensor(f"qsq{tag}", [128, cw], BF16))
            sc["qg2"] = pe(nc.sbuf_tensor(f"qg2{tag}", [128, cw], BF16))
            sc["qg4"] = pe(nc.sbuf_tensor(f"qg4{tag}", [128, cw], BF16))
            sc["qr_"] = pe(nc.sbuf_tensor(f"qr_{tag}", [128, cw], F32))
            sc["ohb"] = pe(nc.sbuf_tensor(f"ohb{tag}", [128, cw], BF16))
            sc["qt2"] = sc["qsq"]
            sc["qv2"] = sc["ohb"]
            sc["amax"] = pe(nc.sbuf_tensor(f"amax{tag}", [128, nb], F32))
            sc["aeb"] = pe(nc.sbuf_tensor(f"aeb{tag}", [128, nb], I32))
            sc["asb"] = pe(nc.sbuf_tensor(f"asb{tag}", [128, nb], I32))
            sc["aib"] = pe(nc.sbuf_tensor(f"aib{tag}", [128, nb], I32))
            sc["log"] = pe(nc.sbuf_tensor(f"log{tag}", [128, nb], F32))
            sc["scf"] = pe(nc.sbuf_tensor(f"scf{tag}", [128, nb], F32))
            sc["invv"] = pe(nc.sbuf_tensor(f"invv{tag}", [128, nb], F32))
            if mode == "A":
                sc["ci"] = pe(nc.sbuf_tensor(f"ci{tag}", [128, nb], I32))
                sc["c0"] = pe(nc.sbuf_tensor(f"c0{tag}", [128, nb], F32))
                sc["c1"] = pe(nc.sbuf_tensor(f"c1{tag}", [128, nb], F32))
                sc["scv"] = pe(nc.sbuf_tensor(f"scv{tag}", [128, nb], F32))
            return sc

        def quant_emit(v, src_q, dst_q, sc, mode):
            nb, cw = sc["nb"], sc["cw"]
            qq, qsq, qg2, qg4 = sc["qq"], sc["qsq"], sc["qg2"], sc["qg4"]
            qt2, qv2, qr_ = sc["qt2"], sc["qv2"], sc["qr_"]
            amax, aeb, asb, aib = sc["amax"], sc["aeb"], sc["asb"], sc["aib"]
            src3 = src_q.rearrange("p (b k) -> p b k", k=32)
            v.tensor_reduce(amax[:, :], src3, axis=AX.X, op=OP.max, apply_absolute_value=True)
            v.drain()
            v.tensor_single_scalar(amax[:, :], amax[:, :], 1e-30, op=OP.max)
            v.drain()
            v.tensor_single_scalar(aeb[:, :], amax[:, :].bitcast(I32), 0x7F800000, op=OP.bitwise_and)
            v.drain()
            v.tensor_single_scalar(asb[:, :], aeb[:, :], 2 << 23, op=OP.subtract)
            v.tensor_scalar(aib[:, :], aeb[:, :], -1.0, 2147483648.0, op0=OP.mult, op1=OP.add)
            v.drain()
            sc_bc = bass.AP(asb, 0, [[nb, 128], [1, nb], [0, 32]]).bitcast(F32)
            inv_bc = bass.AP(aib, 0, [[nb, 128], [1, nb], [0, 32]]).bitcast(F32)
            qq3 = qq[:, :].rearrange("p (b k) -> p b k", k=32)
            if mode == "A":
                v.tensor_single_scalar(sc["ci"][:, :], aeb[:, :], 1 << 23, op=OP.add)
                v.drain()
                v.tensor_tensor(sc["c0"][:, :], amax[:, :], aeb[:, :].bitcast(F32), op=OP.is_ge)
                v.tensor_tensor(sc["c1"][:, :], amax[:, :], sc["ci"][:, :].bitcast(F32), op=OP.is_ge)
                v.drain()
                v.tensor_scalar(sc["log"][:, :], sc["c0"][:, :], 0.5, 0.5, op0=OP.mult, op1=OP.add)
                v.tensor_scalar(sc["scf"][:, :], sc["c1"][:, :], 1.0, 1.0, op0=OP.mult, op1=OP.add)
                v.drain()
                v.tensor_tensor(sc["log"][:, :], sc["log"][:, :], sc["scf"][:, :], op=OP.mult)
                v.drain()
                v.tensor_tensor(sc["scv"][:, :], asb[:, :].bitcast(F32), sc["log"][:, :], op=OP.mult)
                v.drain()
                v.tensor_scalar(sc["c0"][:, :], sc["c0"][:, :], -1.0, 2.0, op0=OP.mult, op1=OP.add)
                v.tensor_scalar(sc["c1"][:, :], sc["c1"][:, :], -0.5, 1.0, op0=OP.mult, op1=OP.add)
                v.drain()
                v.tensor_tensor(sc["c0"][:, :], sc["c0"][:, :], sc["c1"][:, :], op=OP.mult)
                v.drain()
                v.tensor_tensor(sc["invv"][:, :], aib[:, :].bitcast(F32), sc["c0"][:, :], op=OP.mult)
                v.drain()
                sc_bc = bass.AP(sc["scv"], 0, [[nb, 128], [1, nb], [0, 32]])
                inv_bc = bass.AP(sc["invv"], 0, [[nb, 128], [1, nb], [0, 32]])
            else:
                oh3 = sc["ohb"][:, :].rearrange("p (b k) -> p b k", k=32)
                hi_bc = bass.AP(asb, 0, [[nb, 128], [1, nb], [0, 32]]).bitcast(F32)
                cand_bc = bass.AP(qtabs, 0, [[64, 128], [0, nb], [1, 32]])
                rlo_bc = bass.AP(qtabs, 32, [[64, 128], [0, nb], [1, 32]])
                v.tensor_tensor(oh3, hi_bc, cand_bc, op=OP.is_equal)
                v.drain()
                v.tensor_tensor(oh3, oh3, rlo_bc, op=OP.mult)
                v.drain()
                v.tensor_reduce(sc["log"][:, :], oh3, axis=AX.X, op=OP.add)
                v.drain()
                v.tensor_scalar(sc["scf"][:, :], sc["log"][:, :], 1.0, None, op0=OP.add)
                v.drain()
                v.tensor_tensor(sc["invv"][:, :], aib[:, :].bitcast(F32), sc["scf"][:, :], op=OP.mult)
                v.drain()
                inv_bc = bass.AP(sc["invv"], 0, [[nb, 128], [1, nb], [0, 32]])
            v.tensor_tensor(qq3, src3, inv_bc, op=OP.mult)
            v.drain()
            v.tensor_scalar(qq[:, :], qq[:, :], 6.0, -6.0, op0=OP.min, op1=OP.max)
            v.drain()
            v.tensor_tensor(qsq[:, :], qq[:, :], qq[:, :], op=OP.mult)
            v.drain()
            v.tensor_single_scalar(qg2[:, :], qsq[:, :], 4.0, op=OP.is_ge)
            v.tensor_single_scalar(qg4[:, :], qsq[:, :], 16.0, op=OP.is_ge)
            v.drain()
            v.scalar_tensor_tensor(qt2[:, :], qg2[:, :], 0.5, qg4[:, :], op0=OP.mult, op1=OP.add)
            v.scalar_tensor_tensor(qv2[:, :], qg4[:, :], 0.5, qg2[:, :], op0=OP.mult, op1=OP.add)
            v.drain()
            v.tensor_scalar(qv2[:, :], qv2[:, :], -1.0, 2.0, op0=OP.mult, op1=OP.add)
            v.drain()
            v.tensor_tensor(qr_[:, :], qq[:, :], qv2[:, :], op=OP.mult)
            v.drain()
            mgc_bc = bass.AP(mgc, 0, [[1, 128], [0, cw]])
            v.tensor_tensor(qq[:, :], qr_[:, :], mgc_bc, op=OP.add)
            v.drain()
            v.tensor_tensor(qq[:, :], qq[:, :], mgc_bc, op=OP.subtract)
            v.drain()
            v.scalar_tensor_tensor(qr_[:, :].rearrange("p (b k) -> p b k", k=32),
                                   qt2[:, :].rearrange("p (b k) -> p b k", k=32), 0.5, sc_bc,
                                   op0=OP.add, op1=OP.mult)
            v.drain()
            v.tensor_tensor(dst_q, qq[:, :], qr_[:, :], op=OP.mult)
            v.drain()

        eC = sC.enter_context
        woT_sb = eC(nc.sbuf_tensor("woT_sb", [128, NDC * D], BF16))
        sC2 = ExitStack()
        y_tm = sC2.enter_context(nc.sbuf_tensor("y_tm", [128, H * TOWN], BF16))
        b_trev2 = g("s_trev").n
        b_ev2 = g("s_ev").n
        b_misc2 = g("sd_misc").n
        with ExitStack() as p2:
            pe = p2.enter_context
            psS = [pe(nc.psum_tensor(f"psS{i}", [128, 512], F32)) for i in range(4)]
            psY = [pe(nc.psum_tensor(f"psY{i}", [128, 129], F32)) for i in range(4)]
            expT = [pe(nc.sbuf_tensor(f"expT{i}", [128, 512], BF16)) for i in range(4)]
            rcp = pe(nc.sbuf_tensor("rcp", [128, 4], F32))
            wtile = [pe(nc.sbuf_tensor(f"wtile{i}", [128, 1024], F32)) for i in range(2)]
            wstage = [pe(nc.sbuf_tensor("wstage0", [128, D], BF16))] * 2
            scW = make_quant_scratch(pe, "W", 1024, mode="W")

            if True:
                @blk.gpsimd
                def _(gp):
                    for dc in range(NDC):
                        gp.wait_ge(g("s_smm").h, max(1, 24 * dc - 240))
                        g("sd_wo").inc(gp.dma_start(out=woT_sb[:, 2048 * dc:2048 * (dc + 1)],
                                                    in_=ap["woT"][128 * dc:128 * (dc + 1), :]), 16)

                @blk.sync
                def _(sy):
                    sy.wait_ge(g("s_smm").h, 1)

                    def w_store(sy, ot):
                        sy.wait_ge(g("s_qw").h, 2 * (ot + 1))
                        g("sd_wq80").inc(sy.dma_start(out=wq8_dram[128 * ot:128 * (ot + 1), :],
                                                      in_=wstage[0][:, :]), 16)

                    for ot in range(NDC):
                        for qh in range(2):
                            idx = 2 * ot + qh
                            if idx >= 2:
                                sy.wait_ge(g("s_qw").h, idx - 1)
                            g(f"sd_wf{idx % 2}").inc(sy.dma_start(out=wtile[idx % 2][:, :],
                                                                  in_=ap["wffn"][128 * ot:128 * (ot + 1),
                                                                                 1024 * qh:1024 * (qh + 1)]), 16)
                        if ot >= 1:
                            w_store(sy, ot - 1)
                    w_store(sy, NDC - 1)

                @blk.tensor
                def _(t):
                    t.wait_ge(g("s_trev").h, b_trev2)
                    t.wait_ge(g("s_ev").h, b_ev2)
                    t.wait_ge(g("s_pre").h, 2)
                    for ich in range(32):
                        l, hh = divmod(ich, 16)
                        gv = hh // 4
                        qbase = TOWN * hh + 512 * l
                        for kt in range(13):
                            if kt < 12:
                                gidx = 12 * ich + kt
                                qoff, w, _, _ = kt_geom(kt)
                                if gidx >= 4:
                                    t.wait_ge(g("s_exp").h, gidx - 3)
                                masked = (kt == 0 or kt >= 8)
                                ins = t.matmul(psS[gidx % 4][:, qoff:qoff + w],
                                               Kfm[:, TKV * gv + 512 * l + 128 * kt:TKV * gv + 512 * l + 128 * (kt + 1)],
                                               Qfm[:, qbase + qoff:qbase + qoff + w],
                                               start=True, stop=not masked)
                                if masked:
                                    mv = mask0 if kt == 0 else maskT1
                                    ins = t.matmul(psS[gidx % 4][:, qoff:qoff + w], ident[:, :],
                                                   mv[:, 0:w], start=False, stop=True)
                                g("s_smm").inc(ins)
                            if kt >= 1:
                                ktp = kt - 1
                                gp_ = 12 * ich + ktp
                                _, _, qlo, qhi = kt_geom(ktp)
                                t.wait_ge(g("s_exp").h, gp_ + 1)
                                vt = 4 * l + ktp
                                ins = None
                                last_has_stop = False
                                for qt in range(qlo, qhi + 1):
                                    if ktp == 0 and ich >= 1:
                                        t.wait_ge(g("s_yev").h, 4 * (ich - 1) + qt + 1)
                                    ins = t.matmul(psY[qt][:, :],
                                                   expT[gp_ % 4][:, 128 * qt:128 * (qt + 1)],
                                                   Vaug[:, NT_KV * 129 * gv + 129 * vt:NT_KV * 129 * gv + 129 * (vt + 1)],
                                                   start=(ktp == 0), stop=(ktp == qt + 8))
                                    last_has_stop = (ktp == qt + 8)
                                    if last_has_stop:
                                        g("s_ystop").inc(ins)
                                if last_has_stop:
                                    g("s_avk").inc(t.drain())
                                else:
                                    g("s_avk").inc(ins)

                @blk.vector
                def _(v):
                    v.wait_ge(g("sd_misc").h, 32)
                    v.wait_ge(g("s_pre").h, 1)
                    for ich in range(32):
                        for qt in range(4):
                            v.wait_ge(g("s_ystop").h, 4 * ich + qt + 1)
                            if ich >= 1:
                                v.wait_ge(g("s_yev").h, 4 * (ich - 1) + qt + 1)
                            v.reciprocal(rcp[:, qt:qt + 1], psY[qt][:, 128:129])
                            g("s_rcp").inc(v.drain())
                        idx = ich
                        ot, qh = divmod(idx, 2)
                        v.wait_ge(g(f"sd_wf{idx % 2}").h, 16 * (idx // 2 + 1))
                        if qh == 0 and ot >= 1:
                            v.wait_ge(g("sd_wq80").h, 16 * ot)
                        quant_emit(v, wtile[idx % 2][:, :], wstage[ot % 2][:, 1024 * qh:1024 * (qh + 1)], scW, "W")
                        g("s_qw").inc(v.drain())

                @blk.scalar
                def _(s):
                    s.wait_ge(g("sd_misc").h, 32)
                    for ich in range(32):
                        l, hh = divmod(ich, 16)
                        for kt in range(12):
                            qoff, w, _, _ = kt_geom(kt)
                            gidx = 12 * ich + kt
                            s.wait_ge(g("s_smm").h, gidx + 1)
                            if gidx >= 4:
                                s.wait_ge(g("s_avk").h, gidx - 3)
                            s.activation(expT[gidx % 4][:, qoff:qoff + w], psS[gidx % 4][:, qoff:qoff + w],
                                         AF.Exp, bias=expb[:, 12 * l + kt:12 * l + kt + 1])
                            g("s_exp").inc(s.drain())
                        for qt in range(4):
                            s.wait_ge(g("s_rcp").h, 4 * ich + qt + 1)
                            tt = 4 * l + qt
                            s.activation(y_tm[:, TOWN * hh + 128 * tt:TOWN * hh + 128 * (tt + 1)],
                                         psY[qt][:, 0:128], AF.Copy, scale=rcp[:, qt:qt + 1])
                            g("s_yev").inc(s.drain())
        sB.close()

        sY = ExitStack()
        y1 = sY.enter_context(nc.sbuf_tensor("y1", [128, NT_OWN * D], F32, side="right"))
        eD = sD.enter_context
        y_fm = eD(nc.sbuf_tensor("y_fm", [128, NDC * TOWN], BF16, side="right"))
        b_yev = g("s_yev").n
        b_tr3, b_trev3 = g("s_tr").n, g("s_trev").n
        with ExitStack() as p3t:
            pe = p3t.enter_context
            psT3 = [pe(nc.psum_tensor(f"psT3{i}", [128, 128], BF16)) for i in range(2)]
            if True:
                @blk.tensor
                def _(t):
                    t.wait_ge(g("s_yev").h, b_yev)
                    for idx in range(128):
                        hh, tt = divmod(idx, 8)
                        if idx >= 2:
                            t.wait_ge(g("s_trev").h, b_trev3 + idx - 1)
                        g("s_tr").inc(t.transpose(psT3[idx % 2][:, :],
                                                  y_tm[:, TOWN * hh + 128 * tt:TOWN * hh + 128 * (tt + 1)],
                                                  ident[:, :]))

                @blk.scalar
                def _(s):
                    s.wait_ge(g("sd_wq80").h, 16 * 16)
                    for idx in range(128):
                        hh, tt = divmod(idx, 8)
                        s.wait_ge(g("s_tr").h, b_tr3 + idx + 1)
                        s.activation(y_fm[:, TOWN * hh + 128 * tt:TOWN * hh + 128 * (tt + 1)],
                                     psT3[idx % 2][:, :], AF.Copy)
                        g("s_trev").inc(s.drain())

        sC2.close()
        sD2 = ExitStack()
        xo_sb = [sD2.enter_context(nc.sbuf_tensor(f"xo_sb{i}", [128, D], F32, side="right")) for i in range(2)]
        z_sb = [sD2.enter_context(nc.sbuf_tensor(f"z_sb{i}", [128, D], F32, side="right")) for i in range(2)]
        b_mm3 = g("s_mm").n
        b_z = g("s_z").n
        b_sq3 = g("s_sq").n
        b_red3 = g("s_red").n
        b_act3 = g("s_act").n
        b_nb = g("s_nb").n
        b_ln = g("s_ln").n
        b_trev4 = g("s_trev").n
        with ExitStack() as p3:
            pe = p3.enter_context
            psO = [pe(nc.psum_tensor(f"psO{i}", [128, 512], F32)) for i in range(2)]
            if True:
                @blk.sync
                def _(sy):
                    sy.wait_ge(g("s_yev").h, b_yev)
                    sy.wait_ge(g("s_tr").h, b_tr3 + 128)
                    sy.wait_ge(g("sd_wq80").h, 16 * 16)
                    for tt in range(NT_OWN):
                        if tt >= 2:
                            sy.wait_ge(g("s_z").h, b_z + 4 * (tt - 1))
                        g(f"sd_xo{tt % 2}").inc(sy.dma_start(out=xo_sb[tt % 2][:, :],
                                                    in_=ap["xown"][128 * tt:128 * (tt + 1), :]), 16)

                @blk.tensor
                def _(t):
                    t.wait_ge(g("sd_wo").h, 16 * NDC)
                    t.wait_ge(g("s_trev").h, b_trev4)
                    for tt in range(NT_OWN):
                        for og in range(4):
                            gi = 4 * tt + og
                            if gi >= 2:
                                t.wait_ge(g("s_z").h, b_z + gi - 1)
                            ins = None
                            for c in range(NDC):
                                ins = t.matmul(psO[gi % 2][:, :],
                                               y_fm[:, TOWN * c + 128 * tt:TOWN * c + 128 * (tt + 1)],
                                               woT_sb[:, 2048 * c + 512 * og:2048 * c + 512 * (og + 1)],
                                               start=(c == 0), stop=(c == NDC - 1))
                            g("s_mm").inc(ins)

                @blk.vector
                def _(v):
                    for tt in range(NT_OWN):
                        buf = tt % 2
                        if tt >= 2:
                            v.wait_ge(g("s_ln").h, b_ln + tt - 1)
                        for og in range(4):
                            gi = 4 * tt + og
                            v.wait_ge(g("s_mm").h, b_mm3 + gi + 1)
                            v.wait_ge(g(f"sd_xo{tt % 2}").h, 16 * (tt // 2 + 1))
                            v.tensor_tensor(z_sb[buf][:, 512 * og:512 * (og + 1)], psO[gi % 2][:, :],
                                            xo_sb[buf][:, 512 * og:512 * (og + 1)], op=OP.add)
                            g("s_z").inc(v.drain())
                        v.tensor_reduce(mini[:, 0:1], z_sb[buf][:, :], axis=AX.X, op=OP.add)
                        v.wait_ge(g("s_sq").h, b_sq3 + tt + 1)
                        v.tensor_reduce(mini[:, 1:2], sqb[:, :], axis=AX.X, op=OP.add)
                        v.drain()
                        v.tensor_scalar(mini[:, 0:1], mini[:, 0:1], 1.0 / D, None, op0=OP.mult)
                        v.tensor_scalar(mini[:, 1:2], mini[:, 1:2], 1.0 / D, None, op0=OP.mult)
                        v.drain()
                        v.tensor_tensor(mini[:, 2:3], mini[:, 0:1], mini[:, 0:1], op=OP.mult)
                        v.drain()
                        v.tensor_tensor(mini[:, 3:4], mini[:, 1:2], mini[:, 2:3], op=OP.subtract)
                        g("s_red").inc(v.drain())
                        v.wait_ge(g("s_act").h, b_act3 + tt + 1)
                        v.scalar_tensor_tensor(mini[:, 6:7], mini[:, 0:1], -1.0, mini[:, 5:6],
                                               op0=OP.mult, op1=OP.mult)
                        g("s_nb").inc(v.drain())

                @blk.scalar
                def _(s):
                    s.wait_ge(g("sd_misc").h, 32)
                    for tt in range(NT_OWN):
                        buf = tt % 2
                        s.wait_ge(g("s_z").h, b_z + 4 * tt + 4)
                        s.activation(sqb[:, :], z_sb[buf][:, :], AF.Square)
                        g("s_sq").inc(s.drain())
                        s.wait_ge(g("s_red").h, b_red3 + tt + 1)
                        s.activation(mini[:, 4:5], mini[:, 3:4], AF.Ln, bias=expb[:, 26:27])
                        s.drain()
                        s.activation(mini[:, 5:6], mini[:, 4:5], AF.Exp, scale=-0.5)
                        g("s_act").inc(s.drain())
                        s.wait_ge(g("s_nb").h, b_nb + tt + 1)
                        s.activation(y1[:, D * tt:D * (tt + 1)], z_sb[buf][:, :], AF.Identity,
                                     scale=mini[:, 5:6], bias=mini[:, 6:7])
                        g("s_ln").inc(s.drain())
        sD2.close()
        sD.close()
        sC.close()

        eE = sE.enter_context
        wffnqT = eE(nc.sbuf_tensor("wffnqT", [128, NDC * D], BF16, side="right"))
        biasrow = eE(nc.sbuf_tensor("biasrow", [1, D], BF16, side="right"))
        a_til = [eE(nc.sbuf_tensor(f"a_til{i}", [128, D], BF16)) for i in range(2)]
        a_fmT = [eE(nc.sbuf_tensor(f"a_fmT{i}", [128, D], BF16)) for i in range(2)]
        z2 = [eE(nc.sbuf_tensor(f"z2_{i}", [128, D], F32)) for i in range(2)]
        scA = make_quant_scratch(eE, "A", 1024)

        b_tr5, b_trev5 = g("s_tr").n, g("s_trev").n
        with ExitStack() as p4:
            pe = p4.enter_context
            psT4 = [pe(nc.psum_tensor(f"psT4{i}", [128, 128], BF16)) for i in range(2)]
            psF = [pe(nc.psum_tensor(f"psF{i}", [128, 512], F32)) for i in range(2)]

            if True:
                @blk.gpsimd
                def _(gp):
                    gp.wait_ge(g("s_ln").h, b_ln + NT_OWN)
                    g("sd_bias").inc(gp.dma_start(out=biasrow[:1, :], in_=ap["fbias"][:1, :]), 16)

                @blk.sync
                def _(sy):
                    sy.wait_ge(g("s_ln").h, b_ln + NT_OWN)
                    sy.wait_ge(g("sd_wq80").h, 16 * 16)
                    for c in range(NDC):
                        g("sd_wqL").inc(sy.dma_start_transpose(
                            out=wffnqT[:, 2048 * c:2048 * (c + 1)],
                            in_=wq8_dram[:, 128 * c:128 * (c + 1)]), 16)
                    for tt in range(NT_OWN):
                        sy.wait_ge(g("s_ln2").h, tt + 1)
                        g(f"sd_out{tt % 2}").inc(sy.dma_start(out=out[128 * tt:128 * (tt + 1), :],
                                                              in_=z2[tt % 2][:, :]), 16)
                    sy.wait_ge(g("sd_out0").h, 16 * (NT_OWN // 2))
                    sy.wait_ge(g("sd_out1").h, 16 * (NT_OWN // 2))

                @blk.vector
                def _(v):
                    v.wait_ge(g("s_pre").h, 1)
                    def a_quant(tt):
                        for qh in range(2):
                            v.wait_ge(g("s_ln").h, b_ln + tt + 1)
                            if tt >= 2 and qh == 0:
                                v.wait_ge(g("s_tr").h, b_tr5 + 16 * (tt - 1))
                            quant_emit(v, y1[:, D * tt + 1024 * qh:D * tt + 1024 * (qh + 1)],
                                       a_til[tt % 2][:, 1024 * qh:1024 * (qh + 1)], scA, "A")
                            g("s_qa").inc(v.drain())

                    def z2_emit(tt):
                        buf = tt % 2
                        if tt >= 2:
                            v.wait_ge(g(f"sd_out{tt % 2}").h, 16 * (tt // 2))
                        for og in range(4):
                            gi = 4 * tt + og
                            v.wait_ge(g("s_fmm").h, gi + 1)
                            v.tensor_tensor(z2[buf][:, 512 * og:512 * (og + 1)], psF[gi % 2][:, :],
                                            y1[:, D * tt + 512 * og:D * tt + 512 * (og + 1)], op=OP.add)
                            g("s_z2").inc(v.drain())
                        v.tensor_reduce(mini[:, 8:9], z2[buf][:, :], axis=AX.X, op=OP.add)
                        v.wait_ge(g("s_sq").h, b_sq3 + NT_OWN + tt + 1)
                        v.tensor_reduce(mini[:, 9:10], sqb[:, :], axis=AX.X, op=OP.add)
                        v.drain()
                        v.tensor_scalar(mini[:, 8:9], mini[:, 8:9], 1.0 / D, None, op0=OP.mult)
                        v.tensor_scalar(mini[:, 9:10], mini[:, 9:10], 1.0 / D, None, op0=OP.mult)
                        v.drain()
                        v.tensor_tensor(mini[:, 10:11], mini[:, 8:9], mini[:, 8:9], op=OP.mult)
                        v.drain()
                        v.tensor_tensor(mini[:, 11:12], mini[:, 9:10], mini[:, 10:11], op=OP.subtract)
                        g("s_red").inc(v.drain())
                        v.wait_ge(g("s_act").h, b_act3 + NT_OWN + tt + 1)
                        v.scalar_tensor_tensor(mini[:, 14:15], mini[:, 8:9], -1.0, mini[:, 13:14],
                                               op0=OP.mult, op1=OP.mult)
                        g("s_nb2").inc(v.drain())

                    for tt in range(NT_OWN):
                        a_quant(tt)
                        if tt >= 1:
                            z2_emit(tt - 1)
                    z2_emit(NT_OWN - 1)

                @blk.tensor
                def _(t):
                    t.wait_ge(g("sd_bias").h, 16)
                    t.wait_ge(g("sd_wqL").h, 16 * NDC)
                    for tt in range(NT_OWN):
                        t.wait_ge(g("s_qa").h, 2 * (tt + 1))
                        for dc in range(NDC):
                            idx = 16 * tt + dc
                            if idx >= 2:
                                t.wait_ge(g("s_trev").h, b_trev5 + idx - 1)
                            g("s_tr").inc(t.transpose(psT4[idx % 2][:, :],
                                                      a_til[tt % 2][:, 128 * dc:128 * (dc + 1)], ident[:, :]))
                        t.wait_ge(g("s_trev").h, b_trev5 + 16 * (tt + 1))
                        for og in range(4):
                            gi = 4 * tt + og
                            if gi >= 2:
                                t.wait_ge(g("s_z2").h, gi - 1)
                            for dc in range(NDC):
                                t.matmul(psF[gi % 2][:, :], a_fmT[tt % 2][:, 128 * dc:128 * (dc + 1)],
                                         wffnqT[:, 2048 * dc + 512 * og:2048 * dc + 512 * (og + 1)],
                                         start=(dc == 0), stop=False)
                            ins = t.matmul(psF[gi % 2][:, :], onescol[:1, :],
                                           biasrow[:1, 512 * og:512 * (og + 1)], start=False, stop=True)
                            g("s_fmm").inc(ins)

                @blk.scalar
                def _(s):
                    s.wait_ge(g("sd_misc").h, 32)
                    for tt in range(NT_OWN):
                        if tt >= 2:
                            s.wait_ge(g("s_fmm").h, 4 * (tt - 1))
                        for dc in range(NDC):
                            idx = 16 * tt + dc
                            s.wait_ge(g("s_tr").h, b_tr5 + idx + 1)
                            s.activation(a_fmT[tt % 2][:, 128 * dc:128 * (dc + 1)], psT4[idx % 2][:, :], AF.Copy)
                            g("s_trev").inc(s.drain())
                        s.wait_ge(g("s_z2").h, 4 * tt + 4)
                        s.activation(sqb[:, :], z2[tt % 2][:, :], AF.Square)
                        g("s_sq").inc(s.drain())
                        s.wait_ge(g("s_red").h, b_red3 + NT_OWN + tt + 1)
                        s.activation(mini[:, 12:13], mini[:, 11:12], AF.Ln, bias=expb[:, 26:27])
                        s.drain()
                        s.activation(mini[:, 13:14], mini[:, 12:13], AF.Exp, scale=-0.5)
                        g("s_act").inc(s.drain())
                        s.wait_ge(g("s_nb2").h, tt + 1)
                        s.activation(z2[tt % 2][:, :], z2[tt % 2][:, :], AF.Identity,
                                     scale=mini[:, 13:14], bias=mini[:, 14:15])
                        g("s_ln2").inc(s.drain())
        sE.close()
        sY.close()
    return nc


_CACHED = {}


def _host_stage(inputs):
    x = np.asarray(inputs["L0_attn_x"], dtype=np.float32)
    wq = np.asarray(inputs["L0_attn_wq"], dtype=np.float32)
    wk = np.asarray(inputs["L0_attn_wk"], dtype=np.float32)
    wv = np.asarray(inputs["L0_attn_wv"], dtype=np.float32)
    wo = np.asarray(inputs["L0_attn_wo"], dtype=np.float32)
    wf = np.ascontiguousarray(np.asarray(inputs["L0_ffn_weight"], dtype=np.float32))
    fb = np.ascontiguousarray(np.asarray(inputs["L0_ffn_bias"], dtype=np.float32).reshape(1, D))

    wqT = np.ascontiguousarray(wq.T)
    wkT = np.ascontiguousarray(wk.T)
    wvT = np.ascontiguousarray(wv.T)
    woT = np.ascontiguousarray(wo.T)

    import jax
    import jax.numpy as jnp
    cpu = jax.local_devices(backend="cpu")[0]
    with jax.default_device(cpu):
        inv_freq = 1.0 / 10000.0 ** (jnp.arange(0, DH, 2, dtype=jnp.float32) / DH)
        freqs = jnp.outer(jnp.arange(TKV, dtype=jnp.float32), inv_freq)
        cos_full = np.asarray(jnp.cos(freqs)).astype(np.float32)
        sin_full = np.asarray(jnp.sin(freqs)).astype(np.float32)
        from jax.experimental import enable_x64
        with enable_x64():
            cand = jnp.arange(-26, 6, dtype=jnp.float64)
            sjax = np.asarray(jnp.exp2(cand), dtype=np.float64)
    hi = 2.0 ** np.arange(-26, 6, dtype=np.float64)
    qtab = np.zeros((1, 64), dtype=np.float32)
    qtab[0, 0:32] = hi.astype(np.float32)
    qtab[0, 32:64] = ((hi - sjax) / hi).astype(np.float32)

    in_maps = []
    for c in range(NCORE):
        b, h = divmod(c, 2)
        kvstart = h * 1024 - 1024
        xb = x[b]
        xT = np.zeros((D, TKV), dtype=np.float32)
        if h == 0:
            xT[:, 1024:] = xb[0:1024].T
        else:
            xT[:, :] = xb.T
        xTt = np.ascontiguousarray(xT.reshape(D, NT_KV, 128).transpose(1, 0, 2))
        xown = np.ascontiguousarray(xb[h * 1024:(h + 1) * 1024])
        pos = np.clip(np.arange(kvstart, kvstart + TKV), 0, None)
        cost = np.ascontiguousarray(cos_full[pos])
        sint = np.ascontiguousarray(sin_full[pos])
        ebias = np.zeros((1, 28), dtype=np.float32)
        ebias[0, 24] = 128e-6
        ebias[0, 25] = 1e-6
        ebias[0, 26] = 1e-5
        ebias[0, 27] = 1.0 / 128.0
        for l in range(2):
            for kt in range(12):
                dead = (h == 0) and ((l == 0 and kt < 8) or (l == 1 and kt < 4))
                ebias[0, 12 * l + kt] = -MSOFT + (KILL if dead else 0.0)
        in_maps.append({
            "xTt": xTt, "xown": xown, "wqT": wqT, "wkT": wkT, "wvT": wvT, "woT": woT,
            "wffn": wf, "fbias": fb, "costab": cost, "sintab": sint, "expbias": ebias,
            "qtab": qtab,
        })
    return in_maps


def kernel(**inputs):
    in_maps = _host_stage(inputs)
    if "nc" not in _CACHED:
        _CACHED["nc"] = build()
    nc = _CACHED["nc"]
    res = run_bass_kernel_spmd(nc, in_maps, core_ids=list(range(NCORE)))
    out = np.empty((4, 2048, 2048), dtype=np.float32)
    for c in range(NCORE):
        b, h = divmod(c, 2)
        out[b, h * 1024:(h + 1) * 1024, :] = res.results[c]["out"]
    return out


# version 27 (final)
# speedup vs baseline: 1.0861x; 1.0861x over previous
import numpy as np
import ml_dtypes
import concourse.bass as bass
import concourse.mybir as mybir
from concourse.bass_utils import run_bass_kernel_spmd
from contextlib import ExitStack

F32 = mybir.dt.float32
BF16 = mybir.dt.bfloat16
I32 = mybir.dt.int32
F8 = mybir.dt.float8e4
AF = mybir.ActivationFunctionType
OP = mybir.AluOpType
AX = mybir.AxisListType

D = 2048
H = 16
KVH = 4
DH = 128
TOWN = 1024
TKV = 2048
NT_OWN = 8
NT_KV = 16
NDC = 16
MSOFT = 12.0
LNEPS = 1e-5
RMSEPS = 1e-6
QSCALE = 0.08838834764831845
LNQS = -2.4259509524980262
MAGIC = 12582912.0
KILL = -30000.0
NCORE = 8


class S:
    def __init__(self, st, nc, name):
        self.h = st.enter_context(nc.semaphore(name))
        self.n = 0

    def inc(self, instr, k=1):
        instr.then_inc(self.h, k)
        self.n += k
        return instr


def kt_geom(kt):
    if kt <= 7:
        return 0, 512, 0, 3
    return 128 * (kt - 8), 512 - 128 * (kt - 8), kt - 8, 3


def build():
    nc = bass.Bass("TRN2", target_bir_lowering=False)
    ap = {}

    def din(name, shape, dt=F32):
        ap[name] = nc.dram_tensor(name, shape, dt, kind="ExternalInput").ap()

    din("xTt", [NT_KV, D, 128])
    din("xown", [TOWN, D])
    din("wqT", [D, D])
    din("wkT", [D, KVH * DH])
    din("wvT", [D, KVH * DH])
    din("woT", [D, D])
    din("wffn", [D, D])
    din("fbias", [1, D])
    din("costab", [TKV, 64])
    din("sintab", [TKV, 64])
    din("expbias", [1, 28])
    din("qtab", [1, 64])
    out = nc.dram_tensor("out", [TOWN, D], F32, kind="ExternalOutput").ap()
    wq8_dram = nc.dram_tensor("wq8_dram", [D, D], BF16, kind="Internal").ap()

    st = ExitStack()
    sA = ExitStack()
    sB = ExitStack()
    sC = ExitStack()
    sD = ExitStack()
    sE = ExitStack()
    with st:
        e = st.enter_context
        ident = e(nc.sbuf_tensor("ident", [128, 128], BF16))
        maskT1 = e(nc.sbuf_tensor("maskT1", [128, 512], BF16))
        mask0 = e(nc.sbuf_tensor("mask0", [128, 512], BF16))
        onescol = e(nc.sbuf_tensor("onescol", [1, 128], BF16))
        expb = e(nc.sbuf_tensor("expb", [128, 28], F32))
        qtabs = e(nc.sbuf_tensor("qtabs", [128, 64], F32))
        mgc = e(nc.sbuf_tensor("mgc", [128, 1], F32))
        mini = e(nc.sbuf_tensor("mini", [128, 64], F32))
        sqb = e(nc.sbuf_tensor("sqb", [128, D], F32))

        sems = {}
        for name in ("sd_misc sd_bias sd_tab sd_x0 sd_x1 sd_wkv sd_wq sd_wo sd_xo0 sd_xo1 sd_wf0 sd_wf1 sd_out0 sd_out1 sd_wq80 sd_wq81 sd_wqL s_pre s_mm s_ev "
                     "s_sq s_red s_act s_rope s_tr s_trev s_smm s_msk s_exp s_avk s_ystop "
                     "s_rcp s_yev s_z s_nb s_ln s_qw s_qa s_fmm s_z2 s_nb2 s_ln2").split():
            sems[name] = S(st, nc, name)
        g = lambda n: sems[n]
        blk = st.enter_context(nc.Block())

        if True:
            @blk.gpsimd
            def _(gp):
                gp.memset(ident[:, :], 0.0)
                gp.memset(maskT1[:, :], 0.0)
                gp.memset(mask0[:, :], 0.0)
                gp.memset(mask0[0:1, :], -10000.0)
                gp.memset(onescol[:, :], 1.0)
                gp.memset(mgc[:, :], 1.0)
                gp.drain()
                gp.tensor_scalar(mgc[:, :], mgc[:, :], 1e-9, -1.0, op0=OP.add, op1=OP.add)
                gp.drain()
                gp.tensor_scalar(mgc[:, :], mgc[:, :], 6.755399441055744e+24, MAGIC,
                                 op0=OP.mult, op1=OP.add)
                gp.drain()
                gp.affine_select(out=ident[:, :], in_=ident[:, :], compare_op=OP.not_equal,
                                 fill=1.0, base=0, pattern=[[-1, 128]], channel_multiplier=1)
                gp.drain()
                gp.affine_select(out=maskT1[:, 0:128], in_=maskT1[:, 0:128], compare_op=OP.is_ge,
                                 fill=-10000.0, base=0, pattern=[[1, 128]], channel_multiplier=-1)
                g("s_pre").inc(gp.drain())
                g("sd_misc").inc(gp.dma_start(out=expb[:, :],
                                              in_=ap["expbias"][0:1, :].broadcast_to((128, 28))), 16)
                g("sd_misc").inc(gp.dma_start(out=qtabs[:, :],
                                              in_=ap["qtab"][0:1, :].broadcast_to((128, 64))), 16)

        eB = sB.enter_context
        Qfm = eB(nc.sbuf_tensor("Qfm", [128, H * TOWN], BF16, side="right"))
        Kfm = eB(nc.sbuf_tensor("Kfm", [128, KVH * TKV], BF16, side="right"))
        Vaug = eB(nc.sbuf_tensor("Vaug", [128, KVH * NT_KV * 129], BF16, side="right"))

        eA = sA.enter_context
        x_tile = [eA(nc.sbuf_tensor(f"x_tile{i}", [128, NDC * 128], BF16, side="right")) for i in range(2)]
        wkT_sb = eA(nc.sbuf_tensor("wkT_sb", [128, NDC * 512], BF16, side="right"))
        wvT_sb = eA(nc.sbuf_tensor("wvT_sb", [128, NDC * 512], BF16, side="right"))
        wqT_sb = eA(nc.sbuf_tensor("wqT_sb", [128, NDC * D], BF16, side="right"))
        cos_sb = eA(nc.sbuf_tensor("cos_sb", [128, NT_KV * 64], F32, side="right"))
        sin_sb = eA(nc.sbuf_tensor("sin_sb", [128, NT_KV * 64], F32, side="right"))
        k_rope = [eA(nc.sbuf_tensor(f"k_rope{i}", [128, 512], BF16, side="right")) for i in range(2)]
        q_rope = [eA(nc.sbuf_tensor(f"q_rope{i}", [128, D], BF16, side="right")) for i in range(2)]
        rtmp1 = eA(nc.sbuf_tensor("rtmp1", [128, 1024], BF16, side="right"))
        rtmp2 = eA(nc.sbuf_tensor("rtmp2", [128, 1024], BF16, side="right"))

        if True:
            @blk.gpsimd
            def _(gp):
                va = Vaug[:, :].rearrange("p (gg t c) -> p (gg t) c", gg=KVH, c=129)
                gp.memset(va[:, :, 128:129], 1.0)
                g("s_pre").inc(gp.drain())
                for dc in range(NDC):
                    g("sd_wkv").inc(gp.dma_start(out=wkT_sb[:, 512 * dc:512 * (dc + 1)],
                                                 in_=ap["wkT"][128 * dc:128 * (dc + 1), :]), 16)
                    g("sd_wkv").inc(gp.dma_start(out=wvT_sb[:, 512 * dc:512 * (dc + 1)],
                                                 in_=ap["wvT"][128 * dc:128 * (dc + 1), :]), 16)
                for tt in range(NT_KV):
                    if tt >= 2:
                        gp.wait_ge(g("s_mm").h, 2 * (tt - 1))
                    dst = x_tile[tt % 2][:, :].rearrange("p (c j) -> p c j", j=128)
                    src = bass.AP(ap["xTt"].tensor, 128 * 2048 * tt, [[128, 128], [128 * 128, NDC], [1, 128]])
                    g(f"sd_x{tt % 2}").inc(gp.dma_start(out=dst, in_=src), 16)
                for dc in range(NDC):
                    g("sd_wq").inc(gp.dma_start(out=wqT_sb[:, 2048 * dc:2048 * (dc + 1)],
                                                in_=ap["wqT"][128 * dc:128 * (dc + 1), :]), 16)
                for tt in range(NT_OWN):
                    gp.wait_ge(g("s_mm").h, 32 + max(0, 4 * (tt - 1)))
                    dst = x_tile[tt % 2][:, :].rearrange("p (c j) -> p c j", j=128)
                    src = bass.AP(ap["xTt"].tensor, 128 * 2048 * (8 + tt), [[128, 128], [128 * 128, NDC], [1, 128]])
                    g(f"sd_x{tt % 2}").inc(gp.dma_start(out=dst, in_=src), 16)

            @blk.sync
            def _(sy):
                ct = cos_sb[:, :].rearrange("p (t c) -> p t c", c=64)
                stt = sin_sb[:, :].rearrange("p (t c) -> p t c", c=64)
                g("sd_tab").inc(sy.dma_start(out=ct, in_=ap["costab"].rearrange("(t p) c -> p t c", p=128)), 16)
                g("sd_tab").inc(sy.dma_start(out=stt, in_=ap["sintab"].rearrange("(t p) c -> p t c", p=128)), 16)

        def rope_ps(v, src_t, pitch, nh, tt_kv, zsl, dst):
            x1 = bass.AP(src_t, 0, [[pitch, 128], [128, nh], [2, 64]])
            x2 = bass.AP(src_t, 1, [[pitch, 128], [128, nh], [2, 64]])
            cosv = bass.AP(cos_sb, 64 * tt_kv, [[NT_KV * 64, 128], [0, nh], [1, 64]])
            sinv = bass.AP(sin_sb, 64 * tt_kv, [[NT_KV * 64, 128], [0, nh], [1, 64]])
            t1 = bass.AP(rtmp1, 0, [[1024, 128], [64, nh], [1, 64]])
            t2 = bass.AP(rtmp2, 0, [[1024, 128], [64, nh], [1, 64]])
            zb = zsl.unsqueeze(2).broadcast_to((128, nh, 64))
            o1 = bass.AP(dst, 0, [[nh * 128, 128], [128, nh], [1, 64]])
            o2 = bass.AP(dst, 64, [[nh * 128, 128], [128, nh], [1, 64]])
            v.tensor_tensor(t1, x1, cosv, op=OP.mult)
            v.tensor_tensor(t2, x2, sinv, op=OP.mult)
            v.drain()
            v.tensor_tensor(t1, t1, t2, op=OP.subtract)
            v.drain()
            v.tensor_tensor(o1, t1, zb, op=OP.mult)
            v.tensor_tensor(t2, x2, cosv, op=OP.mult)
            v.drain()
            v.tensor_tensor(t1, x1, sinv, op=OP.mult)
            v.drain()
            v.tensor_tensor(t1, t1, t2, op=OP.add)
            v.drain()
            v.tensor_tensor(o2, t1, zb, op=OP.mult)

        with ExitStack() as p1t:
            psT = [p1t.enter_context(nc.psum_tensor(f"psT{i}", [128, 128], BF16)) for i in range(2)]

            def k_transpose_group(t, j):
                t.wait_ge(g("s_rope").h, j + 1)
                for g4 in range(4):
                    idx = 4 * j + g4
                    if idx >= 2:
                        t.wait_ge(g("s_trev").h, idx - 1)
                    g("s_tr").inc(t.transpose(psT[idx % 2][:, :],
                                              k_rope[j % 2][:, 128 * g4:128 * (g4 + 1)], ident[:, :]))

            def k_trev_group(s, j):
                for g4 in range(4):
                    idx = 4 * j + g4
                    s.wait_ge(g("s_tr").h, idx + 1)
                    s.activation(Kfm[:, TKV * g4 + 128 * j:TKV * g4 + 128 * (j + 1)],
                                 psT[idx % 2][:, :], AF.Copy)
                    g("s_trev").inc(s.drain())

            with ExitStack() as p1a:
                pe = p1a.enter_context
                psK = [pe(nc.psum_tensor(f"psK{i}", [128, 512], F32)) for i in range(2)]
                psV = [pe(nc.psum_tensor(f"psV{i}", [128, 512], F32)) for i in range(2)]

                if True:
                    @blk.tensor
                    def _(t):
                        t.wait_ge(g("sd_wkv").h, 16 * 2 * NDC)
                        t.wait_ge(g("s_pre").h, 1)
                        for tt in range(NT_KV):
                            buf = tt % 2
                            if tt >= 2:
                                t.wait_ge(g("s_rope").h, tt - 1)
                                t.wait_ge(g("s_ev").h, tt - 1)
                            t.wait_ge(g(f"sd_x{tt % 2}").h, 16 * (tt // 2 + 1))
                            ins = None
                            for dc in range(NDC):
                                ins = t.matmul(psK[buf][:, :], x_tile[buf][:, 128 * dc:128 * (dc + 1)],
                                               wkT_sb[:, 512 * dc:512 * (dc + 1)],
                                               start=(dc == 0), stop=(dc == NDC - 1))
                            g("s_mm").inc(ins)
                            for dc in range(NDC):
                                ins = t.matmul(psV[buf][:, :], x_tile[buf][:, 128 * dc:128 * (dc + 1)],
                                               wvT_sb[:, 512 * dc:512 * (dc + 1)],
                                               start=(dc == 0), stop=(dc == NDC - 1))
                            g("s_mm").inc(ins)
                            if tt >= 1:
                                k_transpose_group(t, tt - 1)
                        k_transpose_group(t, NT_KV - 1)

                    @blk.scalar
                    def _(s):
                        s.wait_ge(g("sd_misc").h, 32)
                        for tt in range(NT_KV):
                            buf = tt % 2
                            s.wait_ge(g("s_mm").h, 2 * tt + 1)
                            s.activation(sqb[:, 0:512], psK[buf][:, :], AF.Square)
                            g("s_sq").inc(s.drain())
                            s.wait_ge(g("s_mm").h, 2 * tt + 2)
                            dstv = bass.AP(Vaug, 129 * tt, [[KVH * NT_KV * 129, 128], [NT_KV * 129, KVH], [1, 128]])
                            srcv = bass.AP(psV[buf], 0, [[512, 128], [128, KVH], [1, 128]])
                            s.activation(dstv, srcv, AF.Copy)
                            g("s_ev").inc(s.drain())
                            s.wait_ge(g("s_red").h, tt + 1)
                            s.activation(mini[:, 8:12], mini[:, 0:4], AF.Ln,
                                         scale=expb[:, 27:28], bias=expb[:, 25:26])
                            s.drain()
                            s.activation(mini[:, 12:16], mini[:, 8:12], AF.Exp, scale=-0.5)
                            g("s_act").inc(s.drain())
                            if tt >= 1:
                                k_trev_group(s, tt - 1)
                        k_trev_group(s, NT_KV - 1)

                    @blk.vector
                    def _(v):
                        v.wait_ge(g("sd_tab").h, 32)
                        for tt in range(NT_KV):
                            buf = tt % 2
                            v.wait_ge(g("s_sq").h, tt + 1)
                            v.tensor_reduce(mini[:, 0:4], sqb[:, 0:512].rearrange("p (h d) -> p h d", d=128),
                                            axis=AX.X, op=OP.add)
                            g("s_red").inc(v.drain())
                            v.wait_ge(g("s_act").h, tt + 1)
                            if tt >= 2:
                                v.wait_ge(g("s_tr").h, 4 * (tt - 1))
                            rope_ps(v, psK[buf], 512, 4, tt, mini[:, 12:16], k_rope[buf])
                            g("s_rope").inc(v.drain())

            bmm, bsq = g("s_mm").n, g("s_sq").n
            bred, bact, brope = g("s_red").n, g("s_act").n, g("s_rope").n
            btr, btrev = g("s_tr").n, g("s_trev").n

            with ExitStack() as p1b:
                psQ = p1b.enter_context(nc.psum_tensor("psQ", [128, 2048], F32))

                def q_transpose_group(t, j):
                    t.wait_ge(g("s_rope").h, brope + j + 1)
                    for g4 in range(16):
                        idx = 16 * j + g4
                        if idx >= 2:
                            t.wait_ge(g("s_trev").h, btrev + idx - 1)
                        g("s_tr").inc(t.transpose(psT[idx % 2][:, :],
                                                  q_rope[j % 2][:, 128 * g4:128 * (g4 + 1)], ident[:, :]))

                def q_trev_group(s, j):
                    for g4 in range(16):
                        idx = 16 * j + g4
                        s.wait_ge(g("s_tr").h, btr + idx + 1)
                        s.activation(Qfm[:, TOWN * g4 + 128 * j:TOWN * g4 + 128 * (j + 1)],
                                     psT[idx % 2][:, :], AF.Copy)
                        g("s_trev").inc(s.drain())

                if True:
                    @blk.tensor
                    def _(t):
                        t.wait_ge(g("sd_wq").h, 16 * NDC)
                        for tt in range(NT_OWN):
                            buf = tt % 2
                            if tt >= 1:
                                t.wait_ge(g("s_rope").h, brope + tt)
                            t.wait_ge(g(f"sd_x{tt % 2}").h, 16 * (8 + tt // 2 + 1))
                            for og in range(4):
                                ins = None
                                for dc in range(NDC):
                                    ins = t.matmul(psQ[:, 512 * og:512 * (og + 1)],
                                                   x_tile[buf][:, 128 * dc:128 * (dc + 1)],
                                                   wqT_sb[:, 2048 * dc + 512 * og:2048 * dc + 512 * (og + 1)],
                                                   start=(dc == 0), stop=(dc == NDC - 1))
                                g("s_mm").inc(ins)
                            if tt >= 1:
                                q_transpose_group(t, tt - 1)
                        q_transpose_group(t, NT_OWN - 1)

                    @blk.scalar
                    def _(s):
                        s.wait_ge(g("sd_misc").h, 32)
                        for tt in range(NT_OWN):
                            s.wait_ge(g("s_mm").h, bmm + 4 * tt + 4)
                            s.activation(sqb[:, :], psQ[:, :], AF.Square)
                            g("s_sq").inc(s.drain())
                            s.wait_ge(g("s_red").h, bred + tt + 1)
                            s.activation(mini[:, 16:32], mini[:, 0:16], AF.Ln, bias=expb[:, 24:25])
                            s.drain()
                            s.activation(mini[:, 48:64], mini[:, 16:32], AF.Exp, scale=-0.5)
                            g("s_act").inc(s.drain())
                            if tt >= 1:
                                q_trev_group(s, tt - 1)
                        q_trev_group(s, NT_OWN - 1)

                    @blk.vector
                    def _(v):
                        for tt in range(NT_OWN):
                            buf = tt % 2
                            v.wait_ge(g("s_sq").h, bsq + tt + 1)
                            v.tensor_reduce(mini[:, 0:16], sqb[:, :].rearrange("p (h d) -> p h d", d=128),
                                            axis=AX.X, op=OP.add)
                            g("s_red").inc(v.drain())
                            v.wait_ge(g("s_act").h, bact + tt + 1)
                            if tt >= 2:
                                v.wait_ge(g("s_tr").h, btr + 16 * (tt - 1))
                            rope_ps(v, psQ, 2048, 16, 8 + tt, mini[:, 48:64], q_rope[buf])
                            g("s_rope").inc(v.drain())
        sA.close()

        def make_quant_scratch(pe, tag, cw=512, mode="A"):
            nb = cw // 32
            sc = {"nb": nb, "cw": cw}
            sc["qq"] = pe(nc.sbuf_tensor(f"qq{tag}", [128, cw], F32))
            sc["qsq"] = pe(nc.sbuf_tensor(f"qsq{tag}", [128, cw], BF16))
            sc["qg2"] = pe(nc.sbuf_tensor(f"qg2{tag}", [128, cw], BF16))
            sc["qg4"] = pe(nc.sbuf_tensor(f"qg4{tag}", [128, cw], BF16))
            sc["qr_"] = pe(nc.sbuf_tensor(f"qr_{tag}", [128, cw], F32))
            sc["ohb"] = pe(nc.sbuf_tensor(f"ohb{tag}", [128, cw], BF16))
            sc["qt2"] = sc["qsq"]
            sc["qv2"] = sc["ohb"]
            sc["amax"] = pe(nc.sbuf_tensor(f"amax{tag}", [128, nb], F32))
            sc["aeb"] = pe(nc.sbuf_tensor(f"aeb{tag}", [128, nb], I32))
            sc["asb"] = pe(nc.sbuf_tensor(f"asb{tag}", [128, nb], I32))
            sc["aib"] = pe(nc.sbuf_tensor(f"aib{tag}", [128, nb], I32))
            sc["log"] = pe(nc.sbuf_tensor(f"log{tag}", [128, nb], F32))
            sc["scf"] = pe(nc.sbuf_tensor(f"scf{tag}", [128, nb], F32))
            sc["invv"] = pe(nc.sbuf_tensor(f"invv{tag}", [128, nb], F32))
            if mode == "A":
                sc["ci"] = pe(nc.sbuf_tensor(f"ci{tag}", [128, nb], I32))
                sc["c0"] = pe(nc.sbuf_tensor(f"c0{tag}", [128, nb], F32))
                sc["c1"] = pe(nc.sbuf_tensor(f"c1{tag}", [128, nb], F32))
                sc["scv"] = pe(nc.sbuf_tensor(f"scv{tag}", [128, nb], F32))
            return sc

        def quant_emit(v, src_q, dst_q, sc, mode):
            nb, cw = sc["nb"], sc["cw"]
            qq, qsq, qg2, qg4 = sc["qq"], sc["qsq"], sc["qg2"], sc["qg4"]
            qt2, qv2, qr_ = sc["qt2"], sc["qv2"], sc["qr_"]
            amax, aeb, asb, aib = sc["amax"], sc["aeb"], sc["asb"], sc["aib"]
            src3 = src_q.rearrange("p (b k) -> p b k", k=32)
            v.tensor_reduce(amax[:, :], src3, axis=AX.X, op=OP.max, apply_absolute_value=True)
            v.drain()
            v.tensor_single_scalar(amax[:, :], amax[:, :], 1e-30, op=OP.max)
            v.drain()
            v.tensor_single_scalar(aeb[:, :], amax[:, :].bitcast(I32), 0x7F800000, op=OP.bitwise_and)
            v.drain()
            v.tensor_single_scalar(asb[:, :], aeb[:, :], 2 << 23, op=OP.subtract)
            v.tensor_scalar(aib[:, :], aeb[:, :], -1.0, 2147483648.0, op0=OP.mult, op1=OP.add)
            v.drain()
            sc_bc = bass.AP(asb, 0, [[nb, 128], [1, nb], [0, 32]]).bitcast(F32)
            inv_bc = bass.AP(aib, 0, [[nb, 128], [1, nb], [0, 32]]).bitcast(F32)
            qq3 = qq[:, :].rearrange("p (b k) -> p b k", k=32)
            if mode == "A":
                v.tensor_single_scalar(sc["ci"][:, :], aeb[:, :], 1 << 23, op=OP.add)
                v.drain()
                v.tensor_tensor(sc["c0"][:, :], amax[:, :], aeb[:, :].bitcast(F32), op=OP.is_ge)
                v.tensor_tensor(sc["c1"][:, :], amax[:, :], sc["ci"][:, :].bitcast(F32), op=OP.is_ge)
                v.drain()
                v.tensor_scalar(sc["log"][:, :], sc["c0"][:, :], 0.5, 0.5, op0=OP.mult, op1=OP.add)
                v.tensor_scalar(sc["scf"][:, :], sc["c1"][:, :], 1.0, 1.0, op0=OP.mult, op1=OP.add)
                v.drain()
                v.tensor_tensor(sc["log"][:, :], sc["log"][:, :], sc["scf"][:, :], op=OP.mult)
                v.drain()
                v.tensor_tensor(sc["scv"][:, :], asb[:, :].bitcast(F32), sc["log"][:, :], op=OP.mult)
                v.drain()
                v.tensor_scalar(sc["c0"][:, :], sc["c0"][:, :], -1.0, 2.0, op0=OP.mult, op1=OP.add)
                v.tensor_scalar(sc["c1"][:, :], sc["c1"][:, :], -0.5, 1.0, op0=OP.mult, op1=OP.add)
                v.drain()
                v.tensor_tensor(sc["c0"][:, :], sc["c0"][:, :], sc["c1"][:, :], op=OP.mult)
                v.drain()
                v.tensor_tensor(sc["invv"][:, :], aib[:, :].bitcast(F32), sc["c0"][:, :], op=OP.mult)
                v.drain()
                sc_bc = bass.AP(sc["scv"], 0, [[nb, 128], [1, nb], [0, 32]])
                inv_bc = bass.AP(sc["invv"], 0, [[nb, 128], [1, nb], [0, 32]])
            else:
                oh3 = sc["ohb"][:, :].rearrange("p (b k) -> p b k", k=32)
                hi_bc = bass.AP(asb, 0, [[nb, 128], [1, nb], [0, 32]]).bitcast(F32)
                cand_bc = bass.AP(qtabs, 0, [[64, 128], [0, nb], [1, 32]])
                rlo_bc = bass.AP(qtabs, 32, [[64, 128], [0, nb], [1, 32]])
                v.tensor_tensor(oh3, hi_bc, cand_bc, op=OP.is_equal)
                v.drain()
                v.tensor_tensor(oh3, oh3, rlo_bc, op=OP.mult)
                v.drain()
                v.tensor_reduce(sc["log"][:, :], oh3, axis=AX.X, op=OP.add)
                v.drain()
                v.tensor_scalar(sc["scf"][:, :], sc["log"][:, :], 1.0, None, op0=OP.add)
                v.drain()
                v.tensor_tensor(sc["invv"][:, :], aib[:, :].bitcast(F32), sc["scf"][:, :], op=OP.mult)
                v.drain()
                inv_bc = bass.AP(sc["invv"], 0, [[nb, 128], [1, nb], [0, 32]])
            v.tensor_tensor(qq3, src3, inv_bc, op=OP.mult)
            v.drain()
            v.tensor_scalar(qq[:, :], qq[:, :], 6.0, -6.0, op0=OP.min, op1=OP.max)
            v.drain()
            v.tensor_tensor(qsq[:, :], qq[:, :], qq[:, :], op=OP.mult)
            v.drain()
            v.tensor_single_scalar(qg2[:, :], qsq[:, :], 4.0, op=OP.is_ge)
            v.tensor_single_scalar(qg4[:, :], qsq[:, :], 16.0, op=OP.is_ge)
            v.drain()
            v.scalar_tensor_tensor(qt2[:, :], qg2[:, :], 0.5, qg4[:, :], op0=OP.mult, op1=OP.add)
            v.drain()
            mgc_bc = bass.AP(mgc, 0, [[1, 128], [0, cw]])
            v.scalar_tensor_tensor(qr_[:, :], qt2[:, :], 0.5, mgc_bc, op0=OP.add, op1=OP.mult)
            v.drain()
            v.tensor_tensor(qq[:, :], qq[:, :], qr_[:, :], op=OP.add)
            v.drain()
            v.tensor_tensor(qq[:, :], qq[:, :], qr_[:, :], op=OP.subtract)
            v.drain()
            v.tensor_tensor(dst_q.rearrange("p (b k) -> p b k", k=32), qq3, sc_bc, op=OP.mult)
            v.drain()

        eC = sC.enter_context
        woT_sb = eC(nc.sbuf_tensor("woT_sb", [128, NDC * D], BF16))
        sC2 = ExitStack()
        y_tm = sC2.enter_context(nc.sbuf_tensor("y_tm", [128, H * TOWN], BF16))
        b_trev2 = g("s_trev").n
        b_ev2 = g("s_ev").n
        b_misc2 = g("sd_misc").n
        with ExitStack() as p2:
            pe = p2.enter_context
            psS = [pe(nc.psum_tensor(f"psS{i}", [128, 512], F32)) for i in range(4)]
            psY = [pe(nc.psum_tensor(f"psY{i}", [128, 129], F32)) for i in range(4)]
            expT = [pe(nc.sbuf_tensor(f"expT{i}", [128, 512], BF16)) for i in range(4)]
            rcp = pe(nc.sbuf_tensor("rcp", [128, 4], F32))
            wtile = [pe(nc.sbuf_tensor(f"wtile{i}", [128, 1024], F32)) for i in range(2)]
            wstage = [pe(nc.sbuf_tensor("wstage0", [128, D], BF16))] * 2
            scW = make_quant_scratch(pe, "W", 1024, mode="W")

            if True:
                @blk.gpsimd
                def _(gp):
                    for dc in range(NDC):
                        gp.wait_ge(g("s_smm").h, max(1, 24 * dc - 240))
                        g("sd_wo").inc(gp.dma_start(out=woT_sb[:, 2048 * dc:2048 * (dc + 1)],
                                                    in_=ap["woT"][128 * dc:128 * (dc + 1), :]), 16)

                @blk.sync
                def _(sy):
                    sy.wait_ge(g("s_smm").h, 1)

                    def w_store(sy, ot):
                        sy.wait_ge(g("s_qw").h, 2 * (ot + 1))
                        g("sd_wq80").inc(sy.dma_start(out=wq8_dram[128 * ot:128 * (ot + 1), :],
                                                      in_=wstage[0][:, :]), 16)

                    for ot in range(NDC):
                        for qh in range(2):
                            idx = 2 * ot + qh
                            if idx >= 2:
                                sy.wait_ge(g("s_qw").h, idx - 1)
                            g(f"sd_wf{idx % 2}").inc(sy.dma_start(out=wtile[idx % 2][:, :],
                                                                  in_=ap["wffn"][128 * ot:128 * (ot + 1),
                                                                                 1024 * qh:1024 * (qh + 1)]), 16)
                        if ot >= 1:
                            w_store(sy, ot - 1)
                    w_store(sy, NDC - 1)

                @blk.tensor
                def _(t):
                    t.wait_ge(g("s_trev").h, b_trev2)
                    t.wait_ge(g("s_ev").h, b_ev2)
                    t.wait_ge(g("s_pre").h, 2)
                    for ich in range(32):
                        l, hh = divmod(ich, 16)
                        gv = hh // 4
                        qbase = TOWN * hh + 512 * l
                        for kt in range(13):
                            if kt < 12:
                                gidx = 12 * ich + kt
                                qoff, w, _, _ = kt_geom(kt)
                                if gidx >= 4:
                                    t.wait_ge(g("s_exp").h, gidx - 3)
                                masked = (kt == 0 or kt >= 8)
                                ins = t.matmul(psS[gidx % 4][:, qoff:qoff + w],
                                               Kfm[:, TKV * gv + 512 * l + 128 * kt:TKV * gv + 512 * l + 128 * (kt + 1)],
                                               Qfm[:, qbase + qoff:qbase + qoff + w],
                                               start=True, stop=not masked)
                                if masked:
                                    mv = mask0 if kt == 0 else maskT1
                                    ins = t.matmul(psS[gidx % 4][:, qoff:qoff + w], ident[:, :],
                                                   mv[:, 0:w], start=False, stop=True)
                                g("s_smm").inc(ins)
                            if kt >= 1:
                                ktp = kt - 1
                                gp_ = 12 * ich + ktp
                                _, _, qlo, qhi = kt_geom(ktp)
                                t.wait_ge(g("s_exp").h, gp_ + 1)
                                vt = 4 * l + ktp
                                ins = None
                                last_has_stop = False
                                for qt in range(qlo, qhi + 1):
                                    if ktp == 0 and ich >= 1:
                                        t.wait_ge(g("s_yev").h, 4 * (ich - 1) + qt + 1)
                                    ins = t.matmul(psY[qt][:, :],
                                                   expT[gp_ % 4][:, 128 * qt:128 * (qt + 1)],
                                                   Vaug[:, NT_KV * 129 * gv + 129 * vt:NT_KV * 129 * gv + 129 * (vt + 1)],
                                                   start=(ktp == 0), stop=(ktp == qt + 8))
                                    last_has_stop = (ktp == qt + 8)
                                    if last_has_stop:
                                        g("s_ystop").inc(ins)
                                if last_has_stop:
                                    g("s_avk").inc(t.drain())
                                else:
                                    g("s_avk").inc(ins)

                @blk.vector
                def _(v):
                    v.wait_ge(g("sd_misc").h, 32)
                    v.wait_ge(g("s_pre").h, 1)
                    for ich in range(32):
                        for qt in range(4):
                            v.wait_ge(g("s_ystop").h, 4 * ich + qt + 1)
                            if ich >= 1:
                                v.wait_ge(g("s_yev").h, 4 * (ich - 1) + qt + 1)
                            v.reciprocal(rcp[:, qt:qt + 1], psY[qt][:, 128:129])
                            g("s_rcp").inc(v.drain())
                        idx = ich
                        ot, qh = divmod(idx, 2)
                        v.wait_ge(g(f"sd_wf{idx % 2}").h, 16 * (idx // 2 + 1))
                        if qh == 0 and ot >= 1:
                            v.wait_ge(g("sd_wq80").h, 16 * ot)
                        quant_emit(v, wtile[idx % 2][:, :], wstage[ot % 2][:, 1024 * qh:1024 * (qh + 1)], scW, "W")
                        g("s_qw").inc(v.drain())

                @blk.scalar
                def _(s):
                    s.wait_ge(g("sd_misc").h, 32)
                    for ich in range(32):
                        l, hh = divmod(ich, 16)
                        for kt in range(12):
                            qoff, w, _, _ = kt_geom(kt)
                            gidx = 12 * ich + kt
                            s.wait_ge(g("s_smm").h, gidx + 1)
                            if gidx >= 4:
                                s.wait_ge(g("s_avk").h, gidx - 3)
                            s.activation(expT[gidx % 4][:, qoff:qoff + w], psS[gidx % 4][:, qoff:qoff + w],
                                         AF.Exp, bias=expb[:, 12 * l + kt:12 * l + kt + 1])
                            g("s_exp").inc(s.drain())
                        for qt in range(4):
                            s.wait_ge(g("s_rcp").h, 4 * ich + qt + 1)
                            tt = 4 * l + qt
                            s.activation(y_tm[:, TOWN * hh + 128 * tt:TOWN * hh + 128 * (tt + 1)],
                                         psY[qt][:, 0:128], AF.Copy, scale=rcp[:, qt:qt + 1])
                            g("s_yev").inc(s.drain())
        sB.close()

        sY = ExitStack()
        y1 = sY.enter_context(nc.sbuf_tensor("y1", [128, NT_OWN * D], F32, side="right"))
        eD = sD.enter_context
        y_fm = eD(nc.sbuf_tensor("y_fm", [128, NDC * TOWN], BF16, side="right"))
        b_yev = g("s_yev").n
        b_tr3, b_trev3 = g("s_tr").n, g("s_trev").n
        with ExitStack() as p3t:
            pe = p3t.enter_context
            psT3 = [pe(nc.psum_tensor(f"psT3{i}", [128, 128], BF16)) for i in range(2)]
            if True:
                @blk.tensor
                def _(t):
                    t.wait_ge(g("s_yev").h, b_yev)
                    for idx in range(128):
                        hh, tt = divmod(idx, 8)
                        if idx >= 2:
                            t.wait_ge(g("s_trev").h, b_trev3 + idx - 1)
                        g("s_tr").inc(t.transpose(psT3[idx % 2][:, :],
                                                  y_tm[:, TOWN * hh + 128 * tt:TOWN * hh + 128 * (tt + 1)],
                                                  ident[:, :]))

                @blk.scalar
                def _(s):
                    s.wait_ge(g("sd_wq80").h, 16 * 16)
                    for idx in range(128):
                        hh, tt = divmod(idx, 8)
                        s.wait_ge(g("s_tr").h, b_tr3 + idx + 1)
                        s.activation(y_fm[:, TOWN * hh + 128 * tt:TOWN * hh + 128 * (tt + 1)],
                                     psT3[idx % 2][:, :], AF.Copy)
                        g("s_trev").inc(s.drain())

        sC2.close()
        sD2 = ExitStack()
        xo_sb = [sD2.enter_context(nc.sbuf_tensor(f"xo_sb{i}", [128, D], F32, side="right")) for i in range(2)]
        z_sb = [sD2.enter_context(nc.sbuf_tensor(f"z_sb{i}", [128, D], F32, side="right")) for i in range(2)]
        b_mm3 = g("s_mm").n
        b_z = g("s_z").n
        b_sq3 = g("s_sq").n
        b_red3 = g("s_red").n
        b_act3 = g("s_act").n
        b_nb = g("s_nb").n
        b_ln = g("s_ln").n
        b_trev4 = g("s_trev").n
        with ExitStack() as p3:
            pe = p3.enter_context
            psO = [pe(nc.psum_tensor(f"psO{i}", [128, 512], F32)) for i in range(2)]
            if True:
                @blk.sync
                def _(sy):
                    sy.wait_ge(g("s_yev").h, b_yev)
                    sy.wait_ge(g("s_tr").h, b_tr3 + 128)
                    sy.wait_ge(g("sd_wq80").h, 16 * 16)
                    for tt in range(NT_OWN):
                        if tt >= 2:
                            sy.wait_ge(g("s_z").h, b_z + 4 * (tt - 1))
                        g(f"sd_xo{tt % 2}").inc(sy.dma_start(out=xo_sb[tt % 2][:, :],
                                                    in_=ap["xown"][128 * tt:128 * (tt + 1), :]), 16)

                @blk.tensor
                def _(t):
                    t.wait_ge(g("sd_wo").h, 16 * NDC)
                    t.wait_ge(g("s_trev").h, b_trev4)
                    for tt in range(NT_OWN):
                        for og in range(4):
                            gi = 4 * tt + og
                            if gi >= 2:
                                t.wait_ge(g("s_z").h, b_z + gi - 1)
                            ins = None
                            for c in range(NDC):
                                ins = t.matmul(psO[gi % 2][:, :],
                                               y_fm[:, TOWN * c + 128 * tt:TOWN * c + 128 * (tt + 1)],
                                               woT_sb[:, 2048 * c + 512 * og:2048 * c + 512 * (og + 1)],
                                               start=(c == 0), stop=(c == NDC - 1))
                            g("s_mm").inc(ins)

                @blk.vector
                def _(v):
                    for tt in range(NT_OWN):
                        buf = tt % 2
                        if tt >= 2:
                            v.wait_ge(g("s_ln").h, b_ln + tt - 1)
                        for og in range(4):
                            gi = 4 * tt + og
                            v.wait_ge(g("s_mm").h, b_mm3 + gi + 1)
                            v.wait_ge(g(f"sd_xo{tt % 2}").h, 16 * (tt // 2 + 1))
                            v.tensor_tensor(z_sb[buf][:, 512 * og:512 * (og + 1)], psO[gi % 2][:, :],
                                            xo_sb[buf][:, 512 * og:512 * (og + 1)], op=OP.add)
                            g("s_z").inc(v.drain())
                        v.tensor_reduce(mini[:, 0:1], z_sb[buf][:, :], axis=AX.X, op=OP.add)
                        v.wait_ge(g("s_sq").h, b_sq3 + tt + 1)
                        v.tensor_reduce(mini[:, 1:2], sqb[:, :], axis=AX.X, op=OP.add)
                        v.drain()
                        v.tensor_scalar(mini[:, 0:1], mini[:, 0:1], 1.0 / D, None, op0=OP.mult)
                        v.tensor_scalar(mini[:, 1:2], mini[:, 1:2], 1.0 / D, None, op0=OP.mult)
                        v.drain()
                        v.tensor_tensor(mini[:, 2:3], mini[:, 0:1], mini[:, 0:1], op=OP.mult)
                        v.drain()
                        v.tensor_tensor(mini[:, 3:4], mini[:, 1:2], mini[:, 2:3], op=OP.subtract)
                        g("s_red").inc(v.drain())
                        v.wait_ge(g("s_act").h, b_act3 + tt + 1)
                        v.scalar_tensor_tensor(mini[:, 6:7], mini[:, 0:1], -1.0, mini[:, 5:6],
                                               op0=OP.mult, op1=OP.mult)
                        g("s_nb").inc(v.drain())

                @blk.scalar
                def _(s):
                    s.wait_ge(g("sd_misc").h, 32)
                    for tt in range(NT_OWN):
                        buf = tt % 2
                        s.wait_ge(g("s_z").h, b_z + 4 * tt + 4)
                        s.activation(sqb[:, :], z_sb[buf][:, :], AF.Square)
                        g("s_sq").inc(s.drain())
                        s.wait_ge(g("s_red").h, b_red3 + tt + 1)
                        s.activation(mini[:, 4:5], mini[:, 3:4], AF.Ln, bias=expb[:, 26:27])
                        s.drain()
                        s.activation(mini[:, 5:6], mini[:, 4:5], AF.Exp, scale=-0.5)
                        g("s_act").inc(s.drain())
                        s.wait_ge(g("s_nb").h, b_nb + tt + 1)
                        s.activation(y1[:, D * tt:D * (tt + 1)], z_sb[buf][:, :], AF.Identity,
                                     scale=mini[:, 5:6], bias=mini[:, 6:7])
                        g("s_ln").inc(s.drain())
        sD2.close()
        sD.close()
        sC.close()

        eE = sE.enter_context
        wffnqT = eE(nc.sbuf_tensor("wffnqT", [128, NDC * D], BF16, side="right"))
        biasrow = eE(nc.sbuf_tensor("biasrow", [1, D], BF16, side="right"))
        a_til = [eE(nc.sbuf_tensor(f"a_til{i}", [128, D], BF16)) for i in range(2)]
        a_fmT = [eE(nc.sbuf_tensor(f"a_fmT{i}", [128, D], BF16)) for i in range(2)]
        z2 = [eE(nc.sbuf_tensor(f"z2_{i}", [128, D], F32)) for i in range(2)]
        scA = make_quant_scratch(eE, "A", 1024)

        b_tr5, b_trev5 = g("s_tr").n, g("s_trev").n
        with ExitStack() as p4:
            pe = p4.enter_context
            psT4 = [pe(nc.psum_tensor(f"psT4{i}", [128, 128], BF16)) for i in range(2)]
            psF = [pe(nc.psum_tensor(f"psF{i}", [128, 512], F32)) for i in range(2)]

            if True:
                @blk.gpsimd
                def _(gp):
                    gp.wait_ge(g("s_ln").h, b_ln + NT_OWN)
                    g("sd_bias").inc(gp.dma_start(out=biasrow[:1, :], in_=ap["fbias"][:1, :]), 16)

                @blk.sync
                def _(sy):
                    sy.wait_ge(g("s_ln").h, b_ln + NT_OWN)
                    sy.wait_ge(g("sd_wq80").h, 16 * 16)
                    for c in range(NDC):
                        g("sd_wqL").inc(sy.dma_start_transpose(
                            out=wffnqT[:, 2048 * c:2048 * (c + 1)],
                            in_=wq8_dram[:, 128 * c:128 * (c + 1)]), 16)
                    for tt in range(NT_OWN):
                        sy.wait_ge(g("s_ln2").h, tt + 1)
                        g(f"sd_out{tt % 2}").inc(sy.dma_start(out=out[128 * tt:128 * (tt + 1), :],
                                                              in_=z2[tt % 2][:, :]), 16)
                    sy.wait_ge(g("sd_out0").h, 16 * (NT_OWN // 2))
                    sy.wait_ge(g("sd_out1").h, 16 * (NT_OWN // 2))

                @blk.vector
                def _(v):
                    v.wait_ge(g("s_pre").h, 1)
                    def a_quant(tt):
                        for qh in range(2):
                            v.wait_ge(g("s_ln").h, b_ln + tt + 1)
                            if tt >= 2 and qh == 0:
                                v.wait_ge(g("s_tr").h, b_tr5 + 16 * (tt - 1))
                            quant_emit(v, y1[:, D * tt + 1024 * qh:D * tt + 1024 * (qh + 1)],
                                       a_til[tt % 2][:, 1024 * qh:1024 * (qh + 1)], scA, "A")
                            g("s_qa").inc(v.drain())

                    def z2_emit(tt):
                        buf = tt % 2
                        if tt >= 2:
                            v.wait_ge(g(f"sd_out{tt % 2}").h, 16 * (tt // 2))
                        for og in range(4):
                            gi = 4 * tt + og
                            v.wait_ge(g("s_fmm").h, gi + 1)
                            v.tensor_tensor(z2[buf][:, 512 * og:512 * (og + 1)], psF[gi % 2][:, :],
                                            y1[:, D * tt + 512 * og:D * tt + 512 * (og + 1)], op=OP.add)
                            g("s_z2").inc(v.drain())
                        v.tensor_reduce(mini[:, 8:9], z2[buf][:, :], axis=AX.X, op=OP.add)
                        v.wait_ge(g("s_sq").h, b_sq3 + NT_OWN + tt + 1)
                        v.tensor_reduce(mini[:, 9:10], sqb[:, :], axis=AX.X, op=OP.add)
                        v.drain()
                        v.tensor_scalar(mini[:, 8:9], mini[:, 8:9], 1.0 / D, None, op0=OP.mult)
                        v.tensor_scalar(mini[:, 9:10], mini[:, 9:10], 1.0 / D, None, op0=OP.mult)
                        v.drain()
                        v.tensor_tensor(mini[:, 10:11], mini[:, 8:9], mini[:, 8:9], op=OP.mult)
                        v.drain()
                        v.tensor_tensor(mini[:, 11:12], mini[:, 9:10], mini[:, 10:11], op=OP.subtract)
                        g("s_red").inc(v.drain())
                        v.wait_ge(g("s_act").h, b_act3 + NT_OWN + tt + 1)
                        v.scalar_tensor_tensor(mini[:, 14:15], mini[:, 8:9], -1.0, mini[:, 13:14],
                                               op0=OP.mult, op1=OP.mult)
                        g("s_nb2").inc(v.drain())

                    for tt in range(NT_OWN):
                        a_quant(tt)
                        if tt >= 1:
                            z2_emit(tt - 1)
                    z2_emit(NT_OWN - 1)

                @blk.tensor
                def _(t):
                    t.wait_ge(g("sd_bias").h, 16)
                    t.wait_ge(g("sd_wqL").h, 16 * NDC)
                    for tt in range(NT_OWN):
                        t.wait_ge(g("s_qa").h, 2 * (tt + 1))
                        for dc in range(NDC):
                            idx = 16 * tt + dc
                            if idx >= 2:
                                t.wait_ge(g("s_trev").h, b_trev5 + idx - 1)
                            g("s_tr").inc(t.transpose(psT4[idx % 2][:, :],
                                                      a_til[tt % 2][:, 128 * dc:128 * (dc + 1)], ident[:, :]))
                        t.wait_ge(g("s_trev").h, b_trev5 + 16 * (tt + 1))
                        for og in range(4):
                            gi = 4 * tt + og
                            if gi >= 2:
                                t.wait_ge(g("s_z2").h, gi - 1)
                            for dc in range(NDC):
                                t.matmul(psF[gi % 2][:, :], a_fmT[tt % 2][:, 128 * dc:128 * (dc + 1)],
                                         wffnqT[:, 2048 * dc + 512 * og:2048 * dc + 512 * (og + 1)],
                                         start=(dc == 0), stop=False)
                            ins = t.matmul(psF[gi % 2][:, :], onescol[:1, :],
                                           biasrow[:1, 512 * og:512 * (og + 1)], start=False, stop=True)
                            g("s_fmm").inc(ins)

                @blk.scalar
                def _(s):
                    s.wait_ge(g("sd_misc").h, 32)
                    for tt in range(NT_OWN):
                        if tt >= 2:
                            s.wait_ge(g("s_fmm").h, 4 * (tt - 1))
                        for dc in range(NDC):
                            idx = 16 * tt + dc
                            s.wait_ge(g("s_tr").h, b_tr5 + idx + 1)
                            s.activation(a_fmT[tt % 2][:, 128 * dc:128 * (dc + 1)], psT4[idx % 2][:, :], AF.Copy)
                            g("s_trev").inc(s.drain())
                        s.wait_ge(g("s_z2").h, 4 * tt + 4)
                        s.activation(sqb[:, :], z2[tt % 2][:, :], AF.Square)
                        g("s_sq").inc(s.drain())
                        s.wait_ge(g("s_red").h, b_red3 + NT_OWN + tt + 1)
                        s.activation(mini[:, 12:13], mini[:, 11:12], AF.Ln, bias=expb[:, 26:27])
                        s.drain()
                        s.activation(mini[:, 13:14], mini[:, 12:13], AF.Exp, scale=-0.5)
                        g("s_act").inc(s.drain())
                        s.wait_ge(g("s_nb2").h, tt + 1)
                        s.activation(z2[tt % 2][:, :], z2[tt % 2][:, :], AF.Identity,
                                     scale=mini[:, 13:14], bias=mini[:, 14:15])
                        g("s_ln2").inc(s.drain())
        sE.close()
        sY.close()
    return nc


_CACHED = {}


def _host_stage(inputs):
    x = np.asarray(inputs["L0_attn_x"], dtype=np.float32)
    wq = np.asarray(inputs["L0_attn_wq"], dtype=np.float32)
    wk = np.asarray(inputs["L0_attn_wk"], dtype=np.float32)
    wv = np.asarray(inputs["L0_attn_wv"], dtype=np.float32)
    wo = np.asarray(inputs["L0_attn_wo"], dtype=np.float32)
    wf = np.ascontiguousarray(np.asarray(inputs["L0_ffn_weight"], dtype=np.float32))
    fb = np.ascontiguousarray(np.asarray(inputs["L0_ffn_bias"], dtype=np.float32).reshape(1, D))

    wqT = np.ascontiguousarray(wq.T)
    wkT = np.ascontiguousarray(wk.T)
    wvT = np.ascontiguousarray(wv.T)
    woT = np.ascontiguousarray(wo.T)

    import jax
    import jax.numpy as jnp
    cpu = jax.local_devices(backend="cpu")[0]
    with jax.default_device(cpu):
        inv_freq = 1.0 / 10000.0 ** (jnp.arange(0, DH, 2, dtype=jnp.float32) / DH)
        freqs = jnp.outer(jnp.arange(TKV, dtype=jnp.float32), inv_freq)
        cos_full = np.asarray(jnp.cos(freqs)).astype(np.float32)
        sin_full = np.asarray(jnp.sin(freqs)).astype(np.float32)
        from jax.experimental import enable_x64
        with enable_x64():
            cand = jnp.arange(-26, 6, dtype=jnp.float64)
            sjax = np.asarray(jnp.exp2(cand), dtype=np.float64)
    hi = 2.0 ** np.arange(-26, 6, dtype=np.float64)
    qtab = np.zeros((1, 64), dtype=np.float32)
    qtab[0, 0:32] = hi.astype(np.float32)
    qtab[0, 32:64] = ((hi - sjax) / hi).astype(np.float32)

    in_maps = []
    for c in range(NCORE):
        b, h = divmod(c, 2)
        kvstart = h * 1024 - 1024
        xb = x[b]
        xT = np.zeros((D, TKV), dtype=np.float32)
        if h == 0:
            xT[:, 1024:] = xb[0:1024].T
        else:
            xT[:, :] = xb.T
        xTt = np.ascontiguousarray(xT.reshape(D, NT_KV, 128).transpose(1, 0, 2))
        xown = np.ascontiguousarray(xb[h * 1024:(h + 1) * 1024])
        pos = np.clip(np.arange(kvstart, kvstart + TKV), 0, None)
        cost = np.ascontiguousarray(cos_full[pos])
        sint = np.ascontiguousarray(sin_full[pos])
        ebias = np.zeros((1, 28), dtype=np.float32)
        ebias[0, 24] = 128e-6
        ebias[0, 25] = 1e-6
        ebias[0, 26] = 1e-5
        ebias[0, 27] = 1.0 / 128.0
        for l in range(2):
            for kt in range(12):
                dead = (h == 0) and ((l == 0 and kt < 8) or (l == 1 and kt < 4))
                ebias[0, 12 * l + kt] = -MSOFT + (KILL if dead else 0.0)
        in_maps.append({
            "xTt": xTt, "xown": xown, "wqT": wqT, "wkT": wkT, "wvT": wvT, "woT": woT,
            "wffn": wf, "fbias": fb, "costab": cost, "sintab": sint, "expbias": ebias,
            "qtab": qtab,
        })
    return in_maps


def kernel(**inputs):
    in_maps = _host_stage(inputs)
    if "nc" not in _CACHED:
        _CACHED["nc"] = build()
    nc = _CACHED["nc"]
    res = run_bass_kernel_spmd(nc, in_maps, core_ids=list(range(NCORE)))
    out = np.empty((4, 2048, 2048), dtype=np.float32)
    for c in range(NCORE):
        b, h = divmod(c, 2)
        out[b, h * 1024:(h + 1) * 1024, :] = res.results[c]["out"]
    return out
